# version 19
# baseline (speedup 1.0000x reference)
import math
import contextlib
import numpy as np
import ml_dtypes

import concourse.bass as bass
import concourse.mybir as mybir
from concourse.bass_utils import run_bass_kernel_spmd

F32 = mybir.dt.float32
BF16 = mybir.dt.bfloat16
I32 = mybir.dt.int32
OP = mybir.AluOpType
AF = mybir.ActivationFunctionType
bfloat16 = ml_dtypes.bfloat16

B, T, E = 2, 8192, 2048
H, D, CHUNK = 16, 128, 128
INTER = 5504
NIBP = 48
NIB = 43
NFB = 16
NCORE = 8
TOK = (B * T) // NCORE
NTT = TOK // 128
NCH = TOK // CHUNK
IBB = NIBP // NCORE
RMS_EPS = 1e-5
LN_EPS = 1e-5

DEBUG_TAPS = ()
LAST_DEBUG = {}

ROT_PERM = np.concatenate([np.arange(0, 32), np.arange(64, 96),
                           np.arange(32, 64), np.arange(96, 128)])
FULL_PERM = np.concatenate([h * 128 + ROT_PERM for h in range(H)])


def _build_slopes(n):
    def pow2(n):
        start = 2.0 ** (-2.0 ** (-(math.log2(n) - 3)))
        return [start * start ** i for i in range(n)]
    assert math.log2(n).is_integer()
    return np.asarray(pow2(n), dtype=np.float64)


_slopes = _build_slopes(H)
_r = np.exp(-_slopes)
R_C = (_r ** float(CHUNK)).astype(np.float64)


def host_consts(core_id):
    batch = core_id // (NCORE // B)
    strip = core_id % (NCORE // B)
    t0 = strip * TOK

    i = np.arange(CHUNK, dtype=np.float64)
    q_decay = _r[:, None] ** (i[None, :] + 1.0)
    k_decay = _r[:, None] ** (CHUNK - 1.0 - i[None, :])
    diff = i[:, None] - i[None, :]
    dm = np.where(diff >= 0,
                  _r[:, None, None] ** np.where(diff >= 0, diff, 0.0), 0.0)
    dmaskT = np.ascontiguousarray(dm.transpose(0, 2, 1))

    freq = (1.0 / 1024.0) ** np.linspace(0.0, 1.0, D // 4)
    t = np.arange(t0, t0 + TOK, dtype=np.float64)
    theta = t[None, :] * freq[:, None]

    coef = np.zeros((NCORE, H), dtype=np.float64)
    for rp in range(NCORE):
        bp, sp = rp // (NCORE // B), rp % (NCORE // B)
        if bp == batch and sp < strip:
            coef[rp] = R_C ** (NCH * (strip - sp - 1))

    def bf(x):
        return np.ascontiguousarray(x, dtype=np.float32).astype(bfloat16)

    def f32(x):
        return np.ascontiguousarray(x, dtype=np.float32)

    return {
        "c_dmask": bf(dmaskT),
        "c_qdec": bf(np.broadcast_to(q_decay[:, None, :], (H, 128, CHUNK))),
        "c_kdk": f32(k_decay.T),
        "c_rotc": bf(np.concatenate([np.cos(theta), np.cos(theta)], 0)),
        "c_rots": bf(np.concatenate([np.sin(theta), np.sin(theta)], 0)),
        "c_iden": bf(np.eye(128)),
        "c_ones1": f32(np.ones((1, 128))),
        "c_coef": f32(coef.reshape(1, NCORE * H)),
    }


class Buf:
    __slots__ = ("key", "writer", "readers", "pend")

    def __init__(self, key):
        self.key = key
        self.writer = None
        self.readers = []
        self.pend = []


class Opn:
    __slots__ = ("engine", "fn", "deps", "inc_amt", "is_async", "sem", "cum",
                 "needed", "tag", "klass")

    def __init__(self, engine, fn, is_async, tag, klass):
        self.engine = engine
        self.fn = fn
        self.deps = []
        self.is_async = is_async
        self.klass = klass
        self.inc_amt = 16 if klass == "d" else 1
        self.sem = None
        self.cum = None
        self.needed = klass != "c"
        self.tag = tag


ENGINES = ("sync", "scalar", "vector", "tensor", "gpsimd")


class Builder:
    def __init__(self):
        self.ops = []
        self.bufs = {}
        self.final_deps = []

    def buf(self, *key):
        b = self.bufs.get(key)
        if b is None:
            b = Buf(key)
            self.bufs[key] = b
        return b

    def alias(self, newkey, oldkeys):
        nb = self.buf(*newkey)
        for ok in oldkeys:
            ob = self.bufs.get(ok)
            if ob is None:
                continue
            if ob.writer is not None:
                nb.pend.append(ob.writer)
            nb.pend.extend(ob.readers)
            nb.pend.extend(ob.pend)
            ob.writer = None
            ob.readers = []
            ob.pend = []
        return nb

    def alias_group(self, newkeys, oldkeys):
        pend = []
        for ok in oldkeys:
            ob = self.bufs.get(ok)
            if ob is None:
                continue
            if ob.writer is not None:
                pend.append(ob.writer)
            pend.extend(ob.readers)
            pend.extend(ob.pend)
            ob.writer = None
            ob.readers = []
            ob.pend = []
        for nk in newkeys:
            nb = self.buf(*nk)
            nb.pend = nb.pend + list(pend)

    def add(self, engine, fn, reads=(), writes=(), is_async=False, tag="",
            klass=None):
        if klass is None:
            klass = "d" if is_async else "c"
        op = Opn(engine, fn, is_async, tag, klass)
        deps = []
        for b in reads:
            if b.writer is not None:
                deps.append(b.writer)
            b.readers.append(op)
        for b in writes:
            if b.writer is not None:
                deps.append(b.writer)
            deps.extend(b.readers)
            deps.extend(b.pend)
            b.writer = op
            b.readers = []
            b.pend = []
        seen = set()
        for d in deps:
            if d is op or id(d) in seen:
                continue
            seen.add(id(d))
            if (d.engine == "tensor" and engine == "tensor"
                    and not d.is_async and not is_async):
                continue
            op.deps.append(d)
            d.needed = True
        self.ops.append(op)
        return op

    def dma(self, queue, fn, reads=(), writes=(), tag=""):
        return self.add(queue, fn, reads, writes, is_async=True, tag=tag,
                        klass="d")

    def coll(self, fn, reads=(), writes=(), tag=""):
        return self.add("gpsimd", fn, reads, writes, is_async=True, tag=tag,
                        klass="x")

    def finalize_dep(self, op):
        self.final_deps.append(op)
        op.needed = True

    DMA_LANES = 10

    def emit(self, nc, es):
        streams = {e: [] for e in ENGINES}
        for op in self.ops:
            streams[op.engine].append(op)
        sem_handles = {}
        for e in ENGINES:
            dmas = [o for o in streams[e] if o.klass == "d"]
            nlanes = min(self.DMA_LANES, len(dmas)) or 0
            for lane in range(nlanes):
                sem_handles[(e, "d", lane)] = es.enter_context(
                    nc.semaphore(f"s_{e}_d{lane}"))
            lane_prev = {}
            lane_round = {}
            for i, op in enumerate(dmas):
                lane = i % nlanes
                r = lane_round.get(lane, 0)
                lane_round[lane] = r + 1
                op.sem = (e, "d", lane)
                op.cum = 16 * (r + 1)
                prev = lane_prev.get(lane)
                if prev is not None and prev not in op.deps:
                    op.deps.append(prev)
                lane_prev[lane] = op
            for kl in ("c", "x"):
                if any(o.klass == kl and o.needed for o in streams[e]):
                    sem_handles[(e, kl)] = es.enter_context(
                        nc.semaphore(f"s_{e}_{kl}"))
            cnt = {"c": 0, "x": 0}
            for op in streams[e]:
                if op.needed and op.klass in ("c", "x"):
                    cnt[op.klass] += op.inc_amt
                    op.sem = (e, op.klass)
                    op.cum = cnt[op.klass]

        block = es.enter_context(nc.Block())
        builder = self

        def make_emitter(ename, ops):
            def body(eng):
                watermark = {}
                for op in ops:
                    need = {}
                    for d in op.deps:
                        assert d.sem is not None, (d.tag, op.tag)
                        if d.cum > need.get(d.sem, -1):
                            need[d.sem] = d.cum
                    for k, v in sorted(need.items()):
                        if watermark.get(k, 0) >= v:
                            continue
                        eng.wait_ge(sem_handles[k], v)
                        watermark[k] = v
                    inst = op.fn(eng)
                    if op.needed:
                        assert inst is not None, op.tag
                        inst.then_inc(sem_handles[op.sem], op.inc_amt)
                if ename == "gpsimd":
                    need = {}
                    for d in builder.final_deps:
                        if d.cum > need.get(d.sem, -1):
                            need[d.sem] = d.cum
                    for k, v in sorted(need.items()):
                        if watermark.get(k, 0) >= v:
                            continue
                        eng.wait_ge(sem_handles[k], v)
                        watermark[k] = v
            return body

        for e in ENGINES:
            getattr(block, e)(make_emitter(e, streams[e]))


def build_program(debug_taps=()):
    nc = bass.Bass("TRN2", target_bir_lowering=False, debug=False)
    bld = Builder()
    es = contextlib.ExitStack()
    BF = bld.buf

    x_in = nc.dram_tensor("x", [TOK, E], F32, kind="ExternalInput").ap()
    w_ins = {nm: nc.dram_tensor(nm, [E, E], F32, kind="ExternalInput").ap()
             for nm in ("wq", "wk", "wv", "wo")}
    rmsw_in = nc.dram_tensor("rmsw", [1, E], F32, kind="ExternalInput").ap()
    lut_in = nc.dram_tensor("lut", [1, 16], F32, kind="ExternalInput").ap()
    gidx_in = nc.dram_tensor("gidx", [IBB * 128, E], I32, kind="ExternalInput").ap()
    uidx_in = nc.dram_tensor("uidx", [IBB * 128, E], I32, kind="ExternalInput").ap()
    didx_in = nc.dram_tensor("didx", [E, IBB * 128], I32, kind="ExternalInput").ap()
    cns = {}
    for name, shape, dt in [
            ("c_dmask", [H, 128, CHUNK], BF16),
            ("c_qdec", [H, 128, CHUNK], BF16),
            ("c_kdk", [128, H], F32),
            ("c_rotc", [64, TOK], BF16), ("c_rots", [64, TOK], BF16),
            ("c_iden", [128, 128], BF16),
            ("c_ones1", [1, 128], F32), ("c_coef", [1, NCORE * H], F32)]:
        cns[name] = nc.dram_tensor(name, shape, dt, kind="ExternalInput").ap()
    y_out = nc.dram_tensor("y", [TOK, E], F32, kind="ExternalOutput").ap()

    def idram(name, shape, dt):
        return nc.dram_tensor(name, shape, dt).ap()

    w_bf = {nm: idram(nm + "_bf", [E, E], BF16) for nm in ("wq", "wk", "wv", "wo")}
    w_cb = {nm: w_bf[nm].rearrange("(f p) n -> p f n", p=128)
            for nm in ("wq", "wk", "wv", "wo")}
    bld.add("gpsimd", lambda e: e.memset(cf[:, 464:465], LN_EPS),
            writes=[BF("c_eps")], tag="ms_eps")
    bld.add("gpsimd", lambda e: e.memset(cf[:, 465:466], 1.0 / E),
            writes=[BF("c_invE")], tag="ms_invE")
    bld.add("gpsimd", lambda e: e.memset(cf[:, 466:467], 1.0),
            writes=[BF("c_one128")], tag="ms_one")
    for nm in ("wk", "wv", "wq", "wo"):
        bld.dma("gpsimd",
                (lambda nm: lambda e: e.dma_start(out=w_bf[nm], in_=w_ins[nm]))(nm),
                writes=[BF("w_bf", nm)], tag=f"wcast_{nm}")
    mat_sh = {nm: idram("mat" + nm + "_sh", [IBB * 128, E], BF16)
              for nm in ("g", "u", "d")}
    mat = {nm: idram("mat" + nm, [NIBP * 128, E], BF16) for nm in ("g", "u", "d")}
    cc_send = [idram(f"cc_send{i}", [128, 8 * 128], F32) for i in range(2)]
    cc_out = [idram(f"cc_out{i}", [NCORE * 128, 8 * 128], F32) for i in range(2)]
    sp_kt = idram("sp_kt", [H, 128, TOK], BF16)
    sp_kdec = idram("sp_kdec", [H, 128, TOK], BF16)
    sp_v = idram("sp_v", [H, 128, TOK], BF16)
    at_dram = idram("at_dram", [H, 128, TOK], BF16)
    s_dram = idram("s_dram", [NTT, 128, E], BF16)
    h_dram = idram("h_dram", [NIB, 128, TOK], BF16)

    def sb(name, shape, dt):
        return es.enter_context(nc.sbuf_tensor(name, shape, dt))

    xnT = sb("xnT", [128, NFB, 2048], BF16)
    AR = sb("arena", [128, 44032], BF16)
    arB = sb("arB", [128, 6144], F32)
    send = sb("send", [128, H * 128], F32)
    cbuf = sb("cbuf", [128, 10496], BF16)
    cf = sb("cf", [128, 1024], F32)
    stats = sb("stats", [128, 64], F32)
    psf = [es.enter_context(nc.psum_tensor(f"psf{i}", [128, 512], F32))
           for i in range(6)]
    psb = [es.enter_context(nc.psum_tensor(f"psb{i}", [128, 1024], BF16))
           for i in range(2)]

    c_dmask = cbuf[:, 0:2048].rearrange("p (h n) -> p h n", h=H)
    c_qdec = cbuf[:, 2048:4096].rearrange("p (h n) -> p h n", h=H)
    c_rotc = cbuf[0:64, 4096:6144]
    c_rots = cbuf[0:64, 6144:8192]
    c_iden = cbuf[:, 8192:8320]
    v_kvb = cbuf[:, 8448:8576]
    v_qdecT = cbuf[:, 8576:8704]
    v_smask = cbuf[:, 8704:8832]
    c_kdk = cf[:, 0:16]
    c_rmsw = cf[:, 16:32]
    c_lutc = cf[:, 32:48]
    c_coefb = cf[:, 64:192]
    r_ones1 = cf[0:1, 192:320]
    r_lut = cf[0:1, 320:336]
    r_coef = cf[0:1, 336:464]
    r_rms = arB[0:1, 4096:6144]
    c_eps = cf[:, 464:465]
    c_invE = cf[:, 465:466]

    c_one128 = cf[:, 466:467]

    for tag, dst, src, wkey in [
            ("ld_dmask", c_dmask,
             cns["c_dmask"].rearrange("h a n -> a h n"), "c_dmask"),
            ("ld_qdec", c_qdec,
             cns["c_qdec"].rearrange("h a n -> a h n"), "c_qdec"),
            ("ld_rotc", c_rotc, cns["c_rotc"], "c_rotc"),
            ("ld_rots", c_rots, cns["c_rots"], "c_rots"),
            ("ld_iden", c_iden, cns["c_iden"], "c_iden"),
            ("ld_kdk", c_kdk, cns["c_kdk"], "c_kdk"),
            ("ld_ones1", r_ones1, cns["c_ones1"], "r_ones1"),
            ("ld_coef", r_coef, cns["c_coef"], "r_coef"),
            ("ld_lut", r_lut, lut_in, "r_lut"),
            ("ld_rms", r_rms, rmsw_in, "r_rms")]:
        bld.dma("sync", (lambda d, s: lambda e: e.dma_start(out=d, in_=s))(dst, src),
                writes=[BF(wkey)], tag=tag)

    bld.add("tensor", lambda e: e.matmul(psf[5][:, 0:16], r_ones1, r_lut,
                                         start=True, stop=True),
            reads=[BF("r_ones1"), BF("r_lut")], writes=[BF("psf", 5)], tag="bc_lut")
    bld.add("vector", lambda e: e.tensor_copy(c_lutc, psf[5][:, 0:16]),
            reads=[BF("psf", 5)], writes=[BF("c_lutc")], tag="ev_lut")
    bld.add("tensor", lambda e: e.matmul(psf[5][:, 0:128], r_ones1, r_coef,
                                         start=True, stop=True),
            reads=[BF("r_ones1"), BF("r_coef")], writes=[BF("psf", 5)], tag="bc_coef")
    bld.add("vector", lambda e: e.tensor_copy(c_coefb, psf[5][:, 0:128]),
            reads=[BF("psf", 5)], writes=[BF("c_coefb")], tag="ev_coef")
    for h in range(H):
        bld.add("tensor",
                (lambda h: lambda e: e.transpose(
                    psf[4][:, h:h + 1], r_rms[:, h * 128:(h + 1) * 128],
                    r_ones1[0:1, 0:1]))(h),
                reads=[BF("r_rms"), BF("r_ones1")], writes=[BF("psf", 4)],
                tag=f"tr_rms{h}")
    bld.add("vector", lambda e: e.tensor_copy(c_rmsw, psf[4][:, 0:16]),
            reads=[BF("psf", 4)], writes=[BF("c_rmsw")], tag="ev_rmsw")

    wstage = AR[:, 0:8192].rearrange("p (j n) -> p j n", j=4)
    xn_bf = AR[:, 8192:10240]
    idxt = [AR[:, 10240:12288], AR[:, 12288:14336]]
    Wt = AR[:, 14336:16384]
    term = AR[:, 16384:18432]
    stg_gu = AR[:, 18432:20480]
    stg_d = [AR[:, 20480 + k * 2048: 20480 + (k + 1) * 2048] for k in range(6)]
    kTh = AR[:, 32768:34816]
    vTh = AR[:, 34816:36864]
    kdech = AR[:, 36864:38912]
    vtokh = AR[:, 38912:40960]
    whead = AR[:, 40960:43008].rearrange("p (f n) -> p f n", f=NFB)

    def select_lut(dst_ap, src_ap, width, rkey, wkey, tagp, term_ap, term_key):
        bld.add("vector",
                (lambda dst_ap, src_ap: lambda e: e.tensor_scalar(
                    dst_ap, src_ap, 0.0, c_lutc[:, 0:1],
                    OP.is_equal, OP.mult))(dst_ap, src_ap),
                reads=[rkey, BF("c_lutc")], writes=[wkey], tag=tagp + "_s0")
        for j in range(1, 16):
            bld.add("vector",
                    (lambda j, src_ap, width, term_ap: lambda e: e.tensor_scalar(
                        term_ap[:, 0:width], src_ap, float(j), c_lutc[:, j:j + 1],
                        OP.is_equal, OP.mult))(j, src_ap, width, term_ap),
                    reads=[rkey, BF("c_lutc")], writes=[BF(*term_key)],
                    tag=tagp + f"_s{j}")
            bld.add("vector",
                    (lambda dst_ap, width, term_ap: lambda e: e.tensor_tensor(
                        dst_ap, term_ap[:, 0:width], dst_ap, OP.add))(dst_ap, width, term_ap),
                    reads=[BF(*term_key), wkey], writes=[wkey], tag=tagp + f"_a{j}")

    arBb = arB[:].bitcast(BF16)
    gu_wt = [Wt, AR[:, 20480:22528]]
    d_idxt = [arBb[:, 10752:11520], arBb[:, 11520:12288]]
    d_wt = [AR[:, 14336:15104], AR[:, 15104:15872]]
    d_term = AR[:, 16384:17152]

    def m_sel_gu(nm, idx_ap, ibb, mt):
        sl = mt % 2
        bld.dma("gpsimd",
                (lambda idx_ap, ibb, t: lambda e: e.dma_start(
                    out=t, in_=idx_ap[ibb * 128:(ibb + 1) * 128, :]))(
                        idx_ap, ibb, idxt[sl]),
                writes=[BF("idxt", sl)], tag=f"midx_{nm}{ibb}")
        select_lut(gu_wt[sl], idxt[sl], 2048, BF("idxt", sl), BF("Wt", sl),
                   f"msel_{nm}{ibb}", term, ("term",))

    def m_tr_gu(nm, ibb, mt):
        sl = mt % 2
        for fb in range(NFB):
            pb, pk = psb[fb % 2], BF("psb", fb % 2)
            bld.add("tensor",
                    (lambda fb, pb, sl=sl: lambda e: e.transpose(
                        pb[:, 256:384], gu_wt[sl][:, fb * 128:(fb + 1) * 128],
                        c_iden))(fb, pb),
                    reads=[BF("Wt", sl), BF("c_iden")], writes=[pk],
                    tag=f"mtr_{nm}{ibb}_{fb}")
            bld.add("scalar",
                    (lambda fb, pb: lambda e: e.copy(
                        stg_gu[:, fb * 128:(fb + 1) * 128], pb[:, 256:384]))(fb, pb),
                    reads=[pk], writes=[BF("stg_gu")], tag=f"mev_{nm}{ibb}_{fb}")
        bld.dma("scalar",
                (lambda nm, ibb: lambda e: e.dma_start(
                    out=mat_sh[nm][ibb * 128:(ibb + 1) * 128, :],
                    in_=stg_gu))(nm, ibb),
                reads=[BF("stg_gu")], writes=[BF("mat_sh", nm)],
                tag=f"mst_{nm}{ibb}")

    _d_alias = [False]

    def m_sel_d(ot, mt):
        sl = mt % 2
        if not _d_alias[0]:
            _d_alias[0] = True
            bld.alias_group(
                [("d_wt", 0), ("d_wt", 1), ("d_term",), ("d_idxt", 0),
                 ("d_idxt", 1)],
                [("atsl", 0), ("whead", 1)])
        bld.dma("gpsimd",
                (lambda ot, t: lambda e: e.dma_start(
                    out=t, in_=didx_in[ot * 128:(ot + 1) * 128, :]))(
                        ot, d_idxt[sl]),
                writes=[BF("d_idxt", sl)], tag=f"midx_d{ot}")
        select_lut(d_wt[sl], d_idxt[sl], IBB * 128, BF("d_idxt", sl),
                   BF("d_wt", sl), f"msel_d{ot}", d_term, ("d_term",))

    _d_alias2 = [False]

    def m_tr_d(ot, mt):
        sl = mt % 2
        if not _d_alias2[0]:
            _d_alias2[0] = True
            bld.alias_group(
                [("stg_d", k) for k in range(IBB)],
                [("atsl", 1), ("rstdb",), ("wohead", 0)])
        for k in range(IBB):
            pb, pk = psb[k % 2], BF("psb", k % 2)
            bld.add("tensor",
                    (lambda k, pb, sl=sl: lambda e: e.transpose(
                        pb[:, 256:384], d_wt[sl][:, k * 128:(k + 1) * 128],
                        c_iden))(k, pb),
                    reads=[BF("d_wt", sl), BF("c_iden")], writes=[pk],
                    tag=f"mtr_d{ot}_{k}")
            bld.add("scalar",
                    (lambda ot, k, pb: lambda e: e.copy(
                        stg_d[k][:, ot * 128:(ot + 1) * 128], pb[:, 256:384]))(ot, k, pb),
                    reads=[pk], writes=[BF("stg_d", k)], tag=f"mev_d{ot}_{k}")

    def m_ag(nm):
        bld.coll((lambda nm: lambda e: e.collective_compute(
            "AllGather", OP.bypass, replica_groups=[list(range(NCORE))],
            ins=[mat_sh[nm].opt()], outs=[mat[nm].opt()]))(nm),
            reads=[BF("mat_sh", nm)], writes=[BF("mat", nm)], tag=f"ag_{nm}")

    def m_fin():
        for k in range(IBB):
            bld.dma("scalar",
                    (lambda k: lambda e: e.dma_start(
                        out=mat_sh["d"][k * 128:(k + 1) * 128, :],
                        in_=stg_d[k]))(k),
                    reads=[BF("stg_d", k)], writes=[BF("mat_sh", "d")],
                    tag=f"mst_d{k}")
        m_ag("d")

    mats_units = []
    tiles = ([("gu", "g", gidx_in, ibb) for ibb in range(IBB)]
             + [("gu", "u", uidx_in, ibb) for ibb in range(IBB)]
             + [("d", None, None, ot) for ot in range(NFB)])
    sel_u, tr_u = [], []
    for mt, (kind, nm, ap_, i) in enumerate(tiles):
        if kind == "gu":
            sel_u.append((lambda nm=nm, ap_=ap_, i=i, mt=mt: m_sel_gu(nm, ap_, i, mt)))
            tr_u.append((lambda nm=nm, i=i, mt=mt: m_tr_gu(nm, i, mt)))
        else:
            sel_u.append((lambda i=i, mt=mt: m_sel_d(i, mt)))
            tr_u.append((lambda i=i, mt=mt: m_tr_d(i, mt)))
    NGU = 2 * IBB
    mats_tasks = [sel_u[0], sel_u[1]]
    for i in range(NGU):
        mats_tasks.append(tr_u[i])
        if i == IBB - 1:
            mats_tasks.append(lambda: m_ag("g"))
        if i == NGU - 1:
            mats_tasks.append(lambda: m_ag("u"))
        if i + 2 < NGU:
            mats_tasks.append(sel_u[i + 2])
    ffn_units = [sel_u[NGU], sel_u[NGU + 1]]
    for i in range(NGU, len(tiles)):
        ffn_units.append(tr_u[i])
        if i + 2 < len(tiles):
            ffn_units.append(sel_u[i + 2])
    ffn_units.append(m_fin)

    xt = [arB[:, 0:2048], arB[:, 2048:4096]]
    for tt in range(NTT):
        xk = BF("xt", tt % 2)
        bld.dma("sync",
                (lambda tt, b: lambda e: e.dma_start(
                    out=b, in_=x_in[tt * 128:(tt + 1) * 128, :]))(tt, xt[tt % 2]),
                writes=[xk], tag=f"xld{tt}")
        for g in range(4):
            bld.add("vector",
                    (lambda b, g: lambda e: e.bn_stats(
                        stats[:, g * 6:(g + 1) * 6],
                        b[:, g * 512:(g + 1) * 512]))(xt[tt % 2], g),
                    reads=[xk], writes=[BF("stats")], tag=f"bn{tt}_{g}")
        bld.add("vector", lambda e: e.bn_aggr(stats[:, 32:34], stats[:, 0:24]),
                reads=[BF("stats")], writes=[BF("mv")], tag=f"bnagg{tt}")
        bld.add("scalar", lambda e: e.activation(
            stats[:, 34:35], stats[:, 33:34], AF.Sqrt, bias=c_eps),
            reads=[BF("mv"), BF("c_eps")], writes=[BF("sd")], tag=f"sqrt{tt}")
        bld.add("vector", lambda e: e.reciprocal(stats[:, 35:36], stats[:, 34:35]),
                reads=[BF("sd")], writes=[BF("rstd")], tag=f"rcp{tt}")
        bld.add("vector",
                (lambda b: lambda e: e.tensor_scalar(
                    xn_bf, b, stats[:, 32:33], stats[:, 35:36],
                    OP.subtract, OP.mult))(xt[tt % 2]),
                reads=[xk, BF("mv"), BF("rstd")], writes=[BF("xnbf")],
                tag=f"norm{tt}")
        for fb in range(NFB):
            pb, pk = psb[fb % 2], BF("psb", fb % 2)
            bld.add("tensor",
                    (lambda fb, pb: lambda e: e.transpose(
                        pb[:, 0:128], xn_bf[:, fb * 128:(fb + 1) * 128],
                        c_iden))(fb, pb),
                    reads=[BF("xnbf"), BF("c_iden")], writes=[pk],
                    tag=f"trx{tt}_{fb}")
            bld.add("scalar",
                    (lambda fb, tt, pb: lambda e: e.copy(
                        xnT[:, fb, tt * 128:(tt + 1) * 128], pb[:, 0:128]))(fb, tt, pb),
                    reads=[pk], writes=[BF("xnT", fb, tt)], tag=f"evx{tt}_{fb}")

    for _ in range(2):
        mats_tasks.pop(0)()

    S32 = arB[:, 4096:4224]
    bld.alias(("S32",), [("r_rms",)])
    rot_s1 = kdech[32:64, :]
    rot_s2 = kdech[0:32, :]

    def proj_head(dst_ap, wnm, h, dkey, tag, whead_ap=None, whead_key=None,
                  skip_load=False):
        whead_ap = whead if whead_ap is None else whead_ap
        whead_key = ("whead",) if whead_key is None else whead_key
        if not skip_load:
            bld.dma("sync",
                    (lambda wnm, h, whead_ap: lambda e: e.dma_start(
                        out=whead_ap,
                        in_=w_cb[wnm][:, :, h * 128:(h + 1) * 128]))(wnm, h, whead_ap),
                    reads=[BF("w_bf", wnm)], writes=[BF(*whead_key)],
                    tag=tag + "_wld")
        for tg in range(4):
            pf, pk = psf[tg % 2], BF("psf", tg % 2)

            def group(e, tg=tg, pf=pf, whead_ap=whead_ap):
                r = None
                for fb in range(NFB):
                    r = e.matmul(pf[:], whead_ap[:, fb, :],
                                 xnT[:, fb, tg * 512:(tg + 1) * 512],
                                 start=(fb == 0), stop=(fb == NFB - 1))
                return r
            bld.add("tensor", group,
                    reads=[BF(*whead_key)] + [BF("xnT", fb, tg * 4 + q)
                                              for fb in range(NFB) for q in range(4)],
                    writes=[pk], tag=tag + f"_mm{tg}")
            bld.add("scalar",
                    (lambda tg, pf, dst_ap: lambda e: e.copy(
                        dst_ap[:, tg * 512:(tg + 1) * 512], pf[:]))(tg, pf, dst_ap),
                    reads=[pk], writes=[dkey], tag=tag + f"_ev{tg}")

    def rotary(dst_ap, dkey, tag, s1=None, s2=None, s1key=("rot_s1",),
               s2key=("rot_s2",), pre_alias=None):
        s1 = rot_s1 if s1 is None else s1
        s2 = rot_s2 if s2 is None else s2
        if pre_alias is not None:
            bld.alias_group([s1key, s2key], pre_alias)
        A = dst_ap[0:32, :]
        Bq = dst_ap[32:64, :]
        s1k, s2k = BF(*s1key), BF(*s2key)
        bld.add("vector",
                (lambda A, s1: lambda e: e.tensor_tensor(
                    s1, A, c_rots[0:32, :], OP.mult))(A, s1),
                reads=[dkey, BF("c_rots")], writes=[s1k], tag=tag + "_r1")
        bld.add("vector",
                (lambda Bq, s2: lambda e: e.tensor_tensor(
                    s2, Bq, c_rots[32:64, :], OP.mult))(Bq, s2),
                reads=[dkey, BF("c_rots")], writes=[s2k], tag=tag + "_r2")
        bld.add("vector",
                (lambda A: lambda e: e.tensor_tensor(
                    A, A, c_rotc[0:32, :], OP.mult))(A),
                reads=[dkey, BF("c_rotc")], writes=[dkey], tag=tag + "_r3")
        bld.add("vector",
                (lambda A, s2: lambda e: e.tensor_tensor(A, A, s2, OP.add))(A, s2),
                reads=[dkey, s2k], writes=[dkey], tag=tag + "_r4")
        bld.add("vector",
                (lambda Bq: lambda e: e.tensor_tensor(
                    Bq, Bq, c_rotc[32:64, :], OP.mult))(Bq),
                reads=[dkey, BF("c_rotc")], writes=[dkey], tag=tag + "_r5")
        bld.add("vector",
                (lambda Bq, s1: lambda e: e.tensor_tensor(
                    Bq, Bq, s1, OP.subtract))(Bq, s1),
                reads=[dkey, s1k], writes=[dkey], tag=tag + "_r6")

    kThB = arBb[:, 0:2048]
    vThB = arBb[:, 2048:4096]
    kdechB = arBb[:, 4096:6144]
    vtokhB = arBb[:, 6144:8192]
    S32_s = [arB[:, 4096:4224], arB[:, 4224:4352]]
    wheadB = arBb[:, 8704:10752].rearrange("p (f n) -> p f n", f=NFB)
    bld.alias_group(
        [("kTh", 1), ("vTh", 1), ("kdech", 1), ("vtokh", 1), ("whead", 1),
         ("S32", 0), ("S32", 1)],
        [("xt", 0), ("xt", 1), ("r_rms",)])
    bld.alias_group(
        [("kTh", 0), ("vTh", 0), ("kdech", 0), ("vtokh", 0), ("whead", 0)],
        [("kTh",), ("vTh",), ("kdech",), ("vtokh",), ("whead",)])
    kTh_s, vTh_s = [kTh, kThB], [vTh, vThB]
    kdech_s, vtokh_s = [kdech, kdechB], [vtokh, vtokhB]
    whead_s = [whead, wheadB]

    def p1_prologue(h):
        par = h % 2
        kTh_, vTh_, kdech_ = kTh_s[par], vTh_s[par], kdech_s[par]
        kk, vk = BF("kTh", par), BF("vTh", par)
        proj_head(kTh_, "wk", h, kk, f"p1k{h}", whead_s[par], ("whead", par))
        rotary(kTh_, kk, f"rotk{h}", s1=kdech_[32:64, :], s2=kdech_[0:32, :],
               s1key=("rot_s1", par), s2key=("rot_s2", par),
               pre_alias=[("kdech", par)])
        proj_head(vTh_, "wv", h, vk, f"p1v{h}", whead_s[par], ("whead", par))
        bld.alias_group([("kdech", par)], [("rot_s1", par), ("rot_s2", par)])

    def p1_front(h, c):
        par = h % 2
        kTh_, vTh_ = kTh_s[par], vTh_s[par]
        kdech_, vtokh_ = kdech_s[par], vtokh_s[par]
        kk, vk = BF("kTh", par), BF("vTh", par)
        S32_ = S32_s[par]
        pb, pk = psf[4 + par], BF("psf", 4 + par)
        bld.add("tensor",
                (lambda c, pb, kTh_=kTh_: lambda e: e.matmul(
                    pb[:, 0:128], kTh_[:, c * 128:(c + 1) * 128], c_iden,
                    start=True, stop=True))(c, pb),
                reads=[kk, BF("c_iden")], writes=[pk], tag=f"p1trk{h}_{c}")
        bld.add("tensor",
                (lambda c, pb, vTh_=vTh_: lambda e: e.matmul(
                    pb[:, 128:256], vTh_[:, c * 128:(c + 1) * 128], c_iden,
                    start=True, stop=True))(c, pb),
                reads=[vk, BF("c_iden")], writes=[pk], tag=f"p1trv{h}_{c}")
        bld.add("scalar",
                (lambda c, h, pb, kdech_=kdech_: lambda e: e.activation(
                    kdech_[:, c * 128:(c + 1) * 128], pb[:, 0:128],
                    AF.Copy, scale=c_kdk[:, h:h + 1]))(c, h, pb),
                reads=[pk, BF("c_kdk")], writes=[BF("kdech", par)],
                tag=f"p1kd{h}_{c}")
        bld.add("scalar",
                (lambda c, pb, vtokh_=vtokh_: lambda e: e.copy(
                    vtokh_[:, c * 128:(c + 1) * 128], pb[:, 128:256]))(c, pb),
                reads=[pk], writes=[BF("vtokh", par)], tag=f"p1ve{h}_{c}")
    def p1_back(h, c):
        par = h % 2
        kdech_, vtokh_ = kdech_s[par], vtokh_s[par]
        S32_ = S32_s[par]
        kdvS = kdvS_p1[par]
        pf2, pk2 = psf[2 + par], BF("psf", 2 + par)
        bld.add("tensor",
                (lambda c, pf2, kdech_=kdech_, vtokh_=vtokh_: lambda e: e.matmul(
                    pf2[:, 0:128], kdech_[:, c * 128:(c + 1) * 128],
                    vtokh_[:, c * 128:(c + 1) * 128], start=True, stop=True))(
                    c, pf2),
                reads=[BF("kdech", par), BF("vtokh", par)], writes=[pk2],
                tag=f"p1kdv{h}_{c}")
        bld.add("scalar",
                (lambda pf2, kdvS=kdvS: lambda e: e.copy(
                    kdvS, pf2[:, 0:128]))(pf2),
                reads=[pk2], writes=[BF("kdvSp", par)], tag=f"p1ke{h}_{c}")
        if c == 0:
            bld.add("vector",
                    (lambda S32_=S32_, kdvS=kdvS: lambda e: e.tensor_copy(
                        S32_, kdvS))(),
                    reads=[BF("kdvSp", par)], writes=[BF("S32", par)],
                    tag=f"p1si{h}")
        else:
            bld.add("vector",
                    (lambda h, S32_=S32_, kdvS=kdvS:
                     lambda e: e.scalar_tensor_tensor(
                         S32_, S32_, float(R_C[h]), kdvS,
                         OP.mult, OP.add))(h),
                    reads=[BF("kdvSp", par), BF("S32", par)],
                    writes=[BF("S32", par)], tag=f"p1sa{h}_{c}")

    def p1_epilogue(h):
        par = h % 2
        kTh_, kdech_, vtokh_ = kTh_s[par], kdech_s[par], vtokh_s[par]
        bld.add("vector",
                (lambda h, S32_=S32_s[par]: lambda e: e.tensor_copy(
                    send[:, h * 128:(h + 1) * 128], S32_))(h),
                reads=[BF("S32", par)], writes=[BF("send", h)], tag=f"p1se{h}")
        bld.dma("scalar", (lambda h, kTh_=kTh_: lambda e: e.dma_start(
            out=sp_kt[h], in_=kTh_))(h), reads=[BF("kTh", par)],
            writes=[BF("sp_kt", h)], tag=f"spk{h}")
        bld.dma("scalar", (lambda h, kdech_=kdech_: lambda e: e.dma_start(
            out=sp_kdec[h], in_=kdech_))(h), reads=[BF("kdech", par)],
            writes=[BF("sp_kdec", h)], tag=f"spkd{h}")
        bld.dma("scalar", (lambda h, vtokh_=vtokh_: lambda e: e.dma_start(
            out=sp_v[h], in_=vtokh_))(h), reads=[BF("vtokh", par)],
            writes=[BF("sp_v", h)], tag=f"spv{h}")

    kdvS_p1 = [cf[:, 512:640], cf[:, 640:768]]

    for hp in range(H // 2):
        h0, h1 = 2 * hp, 2 * hp + 1
        p1_prologue(h0)
        p1_prologue(h1)
        for c in range(NCH):
            p1_front(h0, c)
            p1_front(h1, c)
            if c > 0:
                p1_back(h0, c - 1)
                p1_back(h1, c - 1)
        p1_back(h0, NCH - 1)
        p1_back(h1, NCH - 1)
        p1_epilogue(h0)
        p1_epilogue(h1)
        if hp in (3, 7):
            hb = hp // 4
            bld.dma("gpsimd",
                    (lambda hb: lambda e: e.dma_start(
                        out=cc_send[hb],
                        in_=send[:, hb * 1024:(hb + 1) * 1024]))(hb),
                    reads=[BF("send", hh) for hh in range(hb * 8, hb * 8 + 8)],
                    writes=[BF("cc_send", hb)], tag=f"ccst{hb}")
            bld.coll((lambda hb: lambda e: e.collective_compute(
                "AllGather", OP.bypass, replica_groups=[list(range(NCORE))],
                ins=[cc_send[hb].opt()],
                outs=[cc_out[hb].opt()]))(hb),
                reads=[BF("cc_send", hb)], writes=[BF("cc_out", hb)],
                tag=f"ag_state{hb}")
        while mats_tasks and len(mats_tasks) > max(0, 24 - (hp + 1) * 3):
            mats_tasks.pop(0)()

    while mats_tasks:
        mats_tasks.pop(0)()

    qTh = [AR[:, 0:2048], AR[:, 2048:4096]]
    kre_s = [AR[:, 4096:6144], AR[:, 32768:34816]]
    kdre_s = [AR[:, 6144:8192], AR[:, 34816:36864]]
    vre_s = [AR[:, 8192:10240], AR[:, 36864:38912]]
    _gb = AR[:, 10240:12288].bitcast(F32).rearrange("p (r n) -> p r n", r=NCORE)
    Gbuf_s = [_gb, _gb]
    atth_s = [AR[:, 12288:14336], AR[:, 38912:40960]]
    bld.alias_group(
        [("qTh", 0), ("qTh", 1), ("kre", 0), ("kdre", 0), ("vre", 0),
         ("Gbuf", 0), ("Gbuf", 1), ("atth", 0), ("rot_s1", 2), ("rot_s2", 2)],
        [("wstage", 0), ("wstage", 1), ("wstage", 2), ("wstage", 3),
         ("xnbf",), ("idxt", 0), ("idxt", 1), ("stg_gu",),
         ("rot_s1", 0), ("rot_s2", 0), ("rot_s1", 1), ("rot_s2", 1)])
    bld.alias_group(
        [("kre", 1), ("kdre", 1), ("vre", 1), ("atth", 1)],
        [("kTh", 0), ("vTh", 0), ("kdech", 0), ("vtokh", 0)])
    rot2_s1 = AR[32:64, 18432:20480]
    rot2_s2 = AR[0:32, 18432:20480]
    qdecf_s = [AR[:, 20480:22528].rearrange("p (c n) -> p c n", c=NCH),
               AR[:, 22528:24576].rearrange("p (c n) -> p c n", c=NCH)]
    bld.alias_group([("qdecf", 0), ("qdecf", 1)], [("Wt", 1)])
    kvb_s = [[cbuf[:, 8448 + (par * 2 + cp) * 128:
                   8448 + (par * 2 + cp + 1) * 128] for cp in range(2)]
             for par in range(2)]
    smask_s = [[cbuf[:, 8960 + (par * 2 + cp) * 128:
                     8960 + (par * 2 + cp + 1) * 128] for cp in range(2)]
               for par in range(2)]

    sq = arB[:, 0:2048]
    ssq = arB[:, 2048:4096]
    bld.alias_group([("sq",), ("ssq",)],
                    [("kTh", 1), ("vTh", 1), ("kdech", 1), ("vtokh", 1)])

    def p2_prologue(h):
        par = h % 2
        qb, qbk = qTh[par], ("qTh", par)
        proj_head(qb, "wq", h, BF(*qbk), f"p2q{h}", whead_s[par], ("whead", par))
        rotary(qb, BF(*qbk), f"rotq{h}", s1=rot2_s1, s2=rot2_s2,
               s1key=("rot_s1", 2), s2key=("rot_s2", 2))
        for c in range(NCH):
            bld.add("vector",
                    (lambda h, c, qb, qd=qdecf_s[par]: lambda e: e.tensor_tensor(
                        qd[:, c, :], qb[:, c * 128:(c + 1) * 128],
                        c_qdec[:, h, :], OP.mult))(h, c, qb),
                    reads=[BF(*qbk), BF("c_qdec")], writes=[BF("qdecf", par)],
                    tag=f"qdf{h}_{c}")
        bld.dma("sync", (lambda h, kre=kre_s[par]: lambda e: e.dma_start(
            out=kre, in_=sp_kt[h]))(h),
                reads=[BF("sp_kt", h)], writes=[BF("kre", par)], tag=f"rlk{h}")
        bld.dma("sync", (lambda h, kd=kdre_s[par]: lambda e: e.dma_start(
            out=kd, in_=sp_kdec[h]))(h),
                reads=[BF("sp_kdec", h)], writes=[BF("kdre", par)], tag=f"rlkd{h}")
        bld.dma("sync", (lambda h, vr=vre_s[par]: lambda e: e.dma_start(
            out=vr, in_=sp_v[h]))(h),
                reads=[BF("sp_v", h)], writes=[BF("vre", par)], tag=f"rlv{h}")
        bld.dma("sync",
                (lambda h, Gbuf=Gbuf_s[par]: lambda e: e.dma_start(
                    out=Gbuf,
                    in_=cc_out[h // 8].rearrange(
                        "(r p) (g n) -> g p r n", p=128, g=8)[h % 8]))(h),
                reads=[BF("cc_out", h // 8)], writes=[BF("Gbuf", 0)],
                tag=f"gld{h}")
        S32_ = S32_s[par]
        bld.add("vector",
                (lambda h, Gbuf=Gbuf_s[par], S32_=S32_: lambda e: e.tensor_scalar(
                    S32_, Gbuf[:, 0, :], c_coefb[:, h:h + 1], None, OP.mult))(h),
                reads=[BF("Gbuf", 0), BF("c_coefb")], writes=[BF("S32", par)],
                tag=f"si0_{h}")
        for r in range(1, NCORE):
            bld.add("vector",
                    (lambda h, r, Gbuf=Gbuf_s[par], S32_=S32_:
                     lambda e: e.scalar_tensor_tensor(
                         S32_, Gbuf[:, r, :], c_coefb[:, r * H + h:r * H + h + 1],
                         S32_, OP.mult, OP.add))(h, r),
                    reads=[BF("Gbuf", 0), BF("c_coefb"), BF("S32", par)],
                    writes=[BF("S32", par)], tag=f"si{r}_{h}")

    kdvS_s = [cf[:, 512:640], cf[:, 640:768]]
    scS_s = [cf[:, 768:896], cf[:, 896:1024]]
    bld.alias_group([("kdvS", 0), ("kdvS", 1)], [("kdvSp", 0), ("kdvSp", 1)])

    def p2_front(h, c):
        par = h % 2
        qb, qbk = qTh[par], ("qTh", par)
        kre = kre_s[par]
        kdre = kdre_s[par].rearrange("p (c n) -> p c n", c=NCH)
        vre = vre_s[par].rearrange("p (c n) -> p c n", c=NCH)
        S32_ = S32_s[par]
        v_kvb, v_smask = kvb_s[par][c % 2], smask_s[par][c % 2]
        kvbk, smk = ("v_kvb", par, c % 2), ("v_smask", par, c % 2)
        kdvS, scS = kdvS_s[par], scS_s[par]
        pfk, pkk = psf[4 + par], BF("psf", 4 + par)
        pf0, pk0 = psf[0 + par], BF("psf", 0 + par)
        if c == 0:
            bld.add("scalar",
                    (lambda v_kvb=v_kvb, S32_=S32_: lambda e: e.copy(
                        v_kvb, S32_))(),
                    reads=[BF("S32", par)], writes=[BF(*kvbk)], tag=f"kvb{h}_{c}")
        else:
            bld.add("vector",
                    (lambda h, v_kvb=v_kvb, S32_=S32_, kdvS=kdvS:
                     lambda e: e.scalar_tensor_tensor(
                         v_kvb, S32_, float(R_C[h]), kdvS,
                         OP.mult, OP.add))(h),
                    reads=[BF("S32", par), BF("kdvS", par)], writes=[BF(*kvbk)],
                    tag=f"kvb{h}_{c}")
            bld.add("vector",
                    (lambda h, S32_=S32_, kdvS=kdvS:
                     lambda e: e.scalar_tensor_tensor(
                         S32_, S32_, float(R_C[h]), kdvS,
                         OP.mult, OP.add))(h),
                    reads=[BF("S32", par), BF("kdvS", par)],
                    writes=[BF("S32", par)], tag=f"sup{h}_{c}")
        bld.add("tensor",
                (lambda c, pf0, qb, kre=kre: lambda e: e.matmul(
                    pf0[:, 0:128], kre[:, c * 128:(c + 1) * 128],
                    qb[:, c * 128:(c + 1) * 128], start=True, stop=True))(
                    c, pf0, qb),
                reads=[BF("kre", par), BF(*qbk)], writes=[pk0], tag=f"sc{h}_{c}")
        bld.add("scalar",
                (lambda pf0, scS=scS: lambda e: e.copy(scS, pf0[:, 0:128]))(pf0),
                reads=[pk0], writes=[BF("scS", par)], tag=f"scev{h}_{c}")
        bld.add("vector",
                (lambda h, v_smask=v_smask, scS=scS: lambda e: e.tensor_tensor(
                    v_smask, scS, c_dmask[:, h, :], OP.mult))(h),
                reads=[BF("scS", par), BF("c_dmask")], writes=[BF(*smk)],
                tag=f"sm{h}_{c}")
        if c < NCH - 1:
            bld.add("tensor",
                    (lambda c, pfk=pfk, kdre=kdre, vre=vre: lambda e: e.matmul(
                        pfk[:, 0:128], kdre[:, c, :], vre[:, c, :],
                        start=True, stop=True))(c),
                    reads=[BF("kdre", par), BF("vre", par)], writes=[pkk],
                    tag=f"kdv2{h}_{c}")
            bld.add("scalar",
                    (lambda pfk=pfk, kdvS=kdvS: lambda e: e.copy(
                        kdvS, pfk[:, 0:128]))(),
                    reads=[pkk], writes=[BF("kdvS", par)], tag=f"kdve{h}_{c}")

    def p2_back(h, c):
        par = h % 2
        vre = vre_s[par].rearrange("p (c n) -> p c n", c=NCH)
        atth = atth_s[par]
        v_smask, v_kvb = smask_s[par][c % 2], kvb_s[par][c % 2]
        smk, kvbk = ("v_smask", par, c % 2), ("v_kvb", par, c % 2)
        pf1, pk1 = psf[2 + par], BF("psf", 2 + par)

        def outgrp(e, c=c, pf1=pf1, vre=vre, v_smask=v_smask, v_kvb=v_kvb,
                   qd=qdecf_s[par]):
            e.matmul(pf1[:, 0:128], vre[:, c, :], v_smask,
                     start=True, stop=False)
            return e.matmul(pf1[:, 0:128], v_kvb, qd[:, c, :],
                            start=False, stop=True)
        bld.add("tensor", outgrp,
                reads=[BF("vre", par), BF(*smk), BF(*kvbk), BF("qdecf", par)],
                writes=[pk1], tag=f"out{h}_{c}")
        bld.add("scalar",
                (lambda c, pf1, atth=atth: lambda e: e.copy(
                    atth[:, c * 128:(c + 1) * 128], pf1[:, 0:128]))(c, pf1),
                reads=[pk1], writes=[BF("atth", par)], tag=f"ate{h}_{c}")

    def p2_epilogue(h):
        par = h % 2
        atth = atth_s[par]
        bld.add("scalar", (lambda atth=atth: lambda e: e.square(sq, atth))(),
                reads=[BF("atth", par)], writes=[BF("sq")], tag=f"sq{h}")
        if h == 0:
            bld.add("vector", lambda e: e.tensor_copy(ssq, sq),
                    reads=[BF("sq")], writes=[BF("ssq")], tag=f"ssqc{h}")
        else:
            bld.add("vector", lambda e: e.tensor_tensor(ssq, sq, ssq, OP.add),
                    reads=[BF("sq"), BF("ssq")], writes=[BF("ssq")],
                    tag=f"ssqa{h}")
        bld.dma("scalar", (lambda h, atth=atth: lambda e: e.dma_start(
            out=at_dram[h], in_=atth))(h),
            reads=[BF("atth", par)], writes=[BF("at_dram", h)], tag=f"atsp{h}")

    for hp in range(H // 2):
        h0, h1 = 2 * hp, 2 * hp + 1
        p2_prologue(h0)
        p2_prologue(h1)
        for c in range(NCH):
            p2_front(h0, c)
            p2_front(h1, c)
            if c > 0:
                p2_back(h0, c - 1)
                p2_back(h1, c - 1)
        p2_back(h0, NCH - 1)
        p2_back(h1, NCH - 1)
        p2_epilogue(h0)
        p2_epilogue(h1)

    for g in range(4):
        bld.add("tensor",
                (lambda g: lambda e: e.matmul(
                    psf[5][0:1, 0:512], c_one128, ssq[:, g * 512:(g + 1) * 512],
                    start=True, stop=True))(g),
                reads=[BF("ssq"), BF("c_one128")], writes=[BF("psf", 5)],
                tag=f"rsum{g}")
        bld.add("vector",
                (lambda g: lambda e: e.tensor_copy(
                    sq[0:1, g * 512:(g + 1) * 512], psf[5][0:1, 0:512]))(g),
                reads=[BF("psf", 5)], writes=[BF("sqrow")], tag=f"rsev{g}")
    bld.add("scalar", lambda e: e.activation(ssq[0:1, :], sq[0:1, :], AF.Sqrt,
                                             bias=c_eps[0:1, :], scale=c_invE[0:1, :]),
            reads=[BF("sqrow"), BF("c_eps"), BF("c_invE")], writes=[BF("ssqrow")],
            tag="rsqrt")
    bld.add("vector", lambda e: e.reciprocal(sq[0:1, :], ssq[0:1, :]),
            reads=[BF("ssqrow")], writes=[BF("sqrow")], tag="rrecip")
    rstdb = AR[:, 28672:30720]
    bld.alias(("rstdb",), [("stg_d", 4)])
    for g in range(4):
        bld.add("tensor",
                (lambda g: lambda e: e.matmul(
                    psf[5][:, 0:512], r_ones1, sq[0:1, g * 512:(g + 1) * 512],
                    start=True, stop=True))(g),
                reads=[BF("sqrow"), BF("r_ones1")], writes=[BF("psf", 5)],
                tag=f"rbc{g}")
        bld.add("vector",
                (lambda g: lambda e: e.tensor_copy(
                    rstdb[:, g * 512:(g + 1) * 512], psf[5][:, 0:512]))(g),
                reads=[BF("psf", 5)], writes=[BF("rstdb")], tag=f"rbce{g}")

    wgb = [AR[:, k * 2048:(k + 1) * 2048].rearrange("p (f n) -> p f n", f=NFB)
           for k in range(2)]
    wub = [AR[:, 4096 + k * 2048: 4096 + (k + 1) * 2048]
           .rearrange("p (f n) -> p f n", f=NFB) for k in range(2)]
    sgb = [cbuf[:, 9472 + k * 512: 9472 + (k + 1) * 512] for k in range(2)]
    hst = [AR[:, 8192 + k * 2048: 8192 + (k + 1) * 2048] for k in range(2)]
    bld.alias_group(
        [("wgb", 0), ("wgb", 1), ("wub", 0), ("wub", 1), ("sgb", 0),
         ("sgb", 1), ("hst", 0), ("hst", 1)],
        [("qTh", 0), ("qTh", 1), ("kre", 0), ("kdre", 0), ("vre", 0),
         ("Gbuf", 0)])

    def ffn_ib(ib, drain_units=True):
        if drain_units:
            while ffn_units and len(ffn_units) > max(0, 35 - (ib + 1)):
                ffn_units.pop(0)()
        for nm, buf, bk in (("g", wgb[ib % 2], ("wgb", ib % 2)),
                            ("u", wub[ib % 2], ("wub", ib % 2))):
            bld.dma("sync",
                    (lambda nm, ib, buf: lambda e: e.dma_start(
                        out=buf.rearrange("p f n -> p (f n)"),
                        in_=mat[nm][ib * 128:(ib + 1) * 128, :]))(nm, ib, buf),
                    reads=[BF("mat", nm)], writes=[BF(*bk)], tag=f"f{nm}l{ib}")
        hk = BF("hst", ib % 2)
        for tg in range(4):
            pfg, pkg = psf[tg % 2], BF("psf", tg % 2)
            pfu, pku = psf[2 + tg % 2], BF("psf", 2 + tg % 2)

            def ggrp(e, buf=wgb[ib % 2], tg=tg, pfg=pfg):
                r = None
                for fb in range(NFB):
                    r = e.matmul(pfg[:], buf[:, fb, :],
                                 xnT[:, fb, tg * 512:(tg + 1) * 512],
                                 start=(fb == 0), stop=(fb == NFB - 1))
                return r
            bld.add("tensor", ggrp,
                    reads=[BF("wgb", ib % 2)] + [BF("xnT", fb, tg * 4 + q)
                                                 for fb in range(NFB)
                                                 for q in range(4)],
                    writes=[pkg], tag=f"fgmm{ib}_{tg}")
            bld.add("scalar",
                    (lambda tg, pfg: lambda e: e.activation(
                        sgb[tg % 2], pfg[:], AF.Silu))(tg, pfg),
                    reads=[pkg], writes=[BF("sgb", tg % 2)], tag=f"fsilu{ib}_{tg}")

            def ugrp(e, buf=wub[ib % 2], tg=tg, pfu=pfu):
                r = None
                for fb in range(NFB):
                    r = e.matmul(pfu[:], buf[:, fb, :],
                                 xnT[:, fb, tg * 512:(tg + 1) * 512],
                                 start=(fb == 0), stop=(fb == NFB - 1))
                return r
            bld.add("tensor", ugrp,
                    reads=[BF("wub", ib % 2)] + [BF("xnT", fb, tg * 4 + q)
                                                 for fb in range(NFB)
                                                 for q in range(4)],
                    writes=[pku], tag=f"fumm{ib}_{tg}")
            bld.add("vector",
                    (lambda tg, pfu, ib: lambda e: e.tensor_tensor(
                        hst[ib % 2][:, tg * 512:(tg + 1) * 512],
                        sgb[tg % 2], pfu[:], OP.mult))(tg, pfu, ib),
                    reads=[BF("sgb", tg % 2), pku], writes=[hk],
                    tag=f"fh{ib}_{tg}")
        bld.dma("scalar",
                (lambda ib, b: lambda e: e.dma_start(
                    out=h_dram[ib], in_=b))(ib, hst[ib % 2]),
                reads=[hk], writes=[BF("h_dram", ib)], tag=f"fhsp{ib}")

    ffn_ib_tasks = [(lambda ib=ib: ffn_ib(ib, drain_units=False))
                    for ib in range(NIB)]

    def drain_ffn(n):
        for _ in range(n):
            if ffn_ib_tasks:
                ffn_ib_tasks.pop(0)()

    atsl = [AR[:, 12288:20480].rearrange("p (f n) -> p f n", f=H),
            AR[:, 20480:28672].rearrange("p (f n) -> p f n", f=H)]
    wohead = [AR[:, 30720 + k * 2048: 30720 + (k + 1) * 2048]
              .rearrange("p (f n) -> p f n", f=NFB) for k in range(2)]
    sTb = [AR[:, 34816 + k * 512: 34816 + (k + 1) * 512] for k in range(2)]
    s_stage = [AR[:, 35840 + k * 2048: 35840 + (k + 1) * 2048] for k in range(4)]
    bld.alias_group(
        [("atsl", 0), ("atsl", 1), ("wohead", 0), ("wohead", 1),
         ("sT", 0), ("sT", 1), ("s_stage", 0), ("s_stage", 1),
         ("s_stage", 2), ("s_stage", 3)],
        [("atth", 0), ("atth", 1), ("Wt", 0), ("term",), ("qdecf", 0),
         ("qdecf", 1), ("rot_s1", 2), ("rot_s2", 2), ("kre", 1), ("kdre", 1),
         ("vre", 1), ("whead", 0)])

    for g4 in range(4):
        ak, ab = BF("atsl", g4 % 2), atsl[g4 % 2]
        bld.dma("sync",
                (lambda g4, ab: lambda e: e.dma_start(
                    out=ab,
                    in_=at_dram[:, :, g4 * 512:(g4 + 1) * 512]
                    .rearrange("h p n -> p h n")))(g4, ab),
                reads=[BF("at_dram", hh) for hh in range(H)], writes=[ak],
                tag=f"atrl{g4}")
        for h in range(H):
            bld.add("vector",
                    (lambda h, ab, g4: lambda e: e.scalar_tensor_tensor(
                        ab[:, h, :], ab[:, h, :], c_rmsw[:, h:h + 1],
                        rstdb[:, g4 * 512:(g4 + 1) * 512], OP.mult, OP.mult))(
                        h, ab, g4),
                    reads=[ak, BF("c_rmsw"), BF("rstdb")], writes=[ak],
                    tag=f"rsc{g4}_{h}")
        for ob in range(NFB):
            wk_, wb_ = BF("wohead", ob % 2), wohead[ob % 2]
            bld.dma("scalar",
                    (lambda ob, wb_: lambda e: e.dma_start(
                        out=wb_,
                        in_=w_cb["wo"][:, :, ob * 128:(ob + 1) * 128]))(ob, wb_),
                    reads=[BF("w_bf", "wo")], writes=[wk_], tag=f"wol{g4}_{ob}")
            pf, pk = psf[ob % 2], BF("psf", ob % 2)

            def wogrp(e, wb_=wb_, ab=ab, pf=pf):
                r = None
                for fb in range(NFB):
                    r = e.matmul(pf[:], wb_[:, fb, :], ab[:, fb, :],
                                 start=(fb == 0), stop=(fb == NFB - 1))
                return r
            bld.add("tensor", wogrp, reads=[wk_, ak], writes=[pk],
                    tag=f"womm{g4}_{ob}")
            sk, sb_ = BF("sT", ob % 2), sTb[ob % 2]
            bld.add("vector",
                    (lambda sb_, pf: lambda e: e.tensor_copy(sb_, pf[:]))(sb_, pf),
                    reads=[pk], writes=[sk], tag=f"woev{g4}_{ob}")
            for q in range(4):
                pb, pbk = psb[q % 2], BF("psb", q % 2)
                bld.add("tensor",
                        (lambda sb_, q, pb: lambda e: e.transpose(
                            pb[:, 0:128], sb_[:, q * 128:(q + 1) * 128],
                            c_iden))(sb_, q, pb),
                        reads=[sk, BF("c_iden")], writes=[pbk],
                        tag=f"wotr{g4}_{ob}_{q}")
                bld.add("vector",
                        (lambda q, ob, pb: lambda e: e.tensor_copy(
                            s_stage[q][:, ob * 128:(ob + 1) * 128],
                            pb[:, 0:128]))(q, ob, pb),
                        reads=[pbk], writes=[BF("s_stage", q)],
                        tag=f"wose{g4}_{ob}_{q}")
        for q in range(4):
            bld.dma("scalar",
                    (lambda g4, q: lambda e: e.dma_start(
                        out=s_dram[g4 * 4 + q], in_=s_stage[q]))(g4, q),
                    reads=[BF("s_stage", q)], writes=[BF("s_dram", g4 * 4 + q)],
                    tag=f"ssp{g4}_{q}")

    _nib_left = [NIB]
    while ffn_ib_tasks:
        ib_done = NIB - len(ffn_ib_tasks)
        while ffn_units and len(ffn_units) > max(0, 35 - (ib_done + 1)):
            ffn_units.pop(0)()
        ffn_ib_tasks.pop(0)()
    while ffn_units:
        ffn_units.pop(0)()
    hhalf = AR[:, 0:NIB * 1024].rearrange("p (i n) -> p i n", i=NIB)
    bld.alias_group(
        [("hhalf", 0), ("hhalf", 1)],
        [("wgb", 0), ("wgb", 1), ("wub", 0), ("wub", 1),
         ("sgb", 0), ("sgb", 1), ("hst", 0), ("hst", 1),
         ("atsl", 0), ("atsl", 1), ("rstdb",),
         ("wohead", 0), ("wohead", 1), ("sT", 0), ("sT", 1),
         ("s_stage", 0), ("s_stage", 1), ("s_stage", 2),
         ("s_stage", 3), ("d_wt", 0), ("d_wt", 1), ("d_term",)]
        + [("stg_d", k) for k in range(IBB)])
    xf = xnT[:].rearrange("p f n -> p (f n)")
    wdob = [xf[:, k * 5504:(k + 1) * 5504].rearrange("p (i n) -> p i n", i=NIB)
            for k in range(2)]
    gT = [xf[:, 11008 + k * 1024: 11008 + (k + 1) * 1024] for k in range(2)]
    gasm = [xf[:, 13056 + k * 2048: 13056 + (k + 1) * 2048] for k in range(8)]
    srb = xf[:, 29440:31488]
    xnT_keys = [("xnT", fb, tt) for fb in range(NFB) for tt in range(NTT)]
    bld.alias_group(
        [("wdob", 0), ("wdob", 1), ("gT", 0), ("gT", 1)]
        + [("gasm", k) for k in range(8)] + [("srb",)],
        xnT_keys)
    xrs = arB[:, 0:2048]
    outs = arB[:, 2048:4096]
    bld.alias_group([("xrs",), ("outs",)], [("sq",), ("ssq",)])

    matd_t = mat["d"].rearrange("(i p) n -> i p n", p=128)
    hq = [AR[:, 0:NIB * 512].rearrange("p (i n) -> p i n", i=NIB),
          AR[:, NIB * 512:NIB * 1024].rearrange("p (i n) -> p i n", i=NIB)]
    for qtr in range(4):
        qp = qtr % 2
        bld.dma("sync",
                (lambda qtr, qp: lambda e: e.dma_start(
                    out=hq[qp],
                    in_=h_dram[:, :, qtr * 512:(qtr + 1) * 512]
                    .rearrange("i p n -> p i n")))(qtr, qp),
                reads=[BF("h_dram", ib) for ib in range(NIB)] if qtr == 0 else [],
                writes=[BF("hhalf", qp)], tag=f"dhl{qtr}")
        for ob in range(NFB):
            wdk, wdb = BF("wdob", ob % 2), wdob[ob % 2]
            bld.dma("sync",
                    (lambda ob, wdb: lambda e: e.dma_start(
                        out=wdb,
                        in_=matd_t[0:NIB, :, ob * 128:(ob + 1) * 128]
                        .rearrange("i p n -> p i n")))(ob, wdb),
                    reads=[BF("mat", "d")], writes=[wdk], tag=f"dwl{qtr}_{ob}")
            pf, pk = psf[ob % 4], BF("psf", ob % 4)

            def dgrp(e, wdb=wdb, qp=qp, pf=pf):
                r = None
                for ib in range(NIB):
                    r = e.matmul(pf[:], wdb[:, ib, :], hq[qp][:, ib, :],
                                 start=(ib == 0), stop=(ib == NIB - 1))
                return r
            bld.add("tensor", dgrp, reads=[wdk, BF("hhalf", qp)],
                    writes=[pk], tag=f"dmm{qtr}_{ob}")
            gk = BF("gT", ob % 2)
            bld.add("scalar",
                    (lambda ob, pf: lambda e: e.activation(
                        gT[ob % 2][:, 0:512], pf[:], AF.Silu))(ob, pf),
                    reads=[pk], writes=[gk], tag=f"dsilu{qtr}_{ob}")
            for q in range(4):
                tl = qp * 4 + q
                pb, pbk = psb[q % 2], BF("psb", q % 2)
                bld.add("tensor",
                        (lambda ob, q, pb: lambda e: e.transpose(
                            pb[:, 0:128],
                            gT[ob % 2][:, q * 128:(q + 1) * 128], c_iden))(ob, q, pb),
                        reads=[gk, BF("c_iden")], writes=[pbk],
                        tag=f"dtr{qtr}_{ob}_{q}")
                bld.add("vector",
                        (lambda tl, ob, pb: lambda e: e.tensor_copy(
                            gasm[tl][:, ob * 128:(ob + 1) * 128],
                            pb[:, 0:128]))(tl, ob, pb),
                        reads=[pbk], writes=[BF("gasm", tl)],
                        tag=f"dge{qtr}_{ob}_{q}")
        for q in range(4):
            tl = qp * 4 + q
            tt = qtr * 4 + q
            bld.dma("scalar",
                    (lambda tt: lambda e: e.dma_start(
                        out=srb, in_=s_dram[tt]))(tt),
                    reads=[BF("s_dram", tt)], writes=[BF("srb")],
                    tag=f"dsl{tt}")
            bld.dma("sync",
                    (lambda tt: lambda e: e.dma_start(
                        out=xrs, in_=x_in[tt * 128:(tt + 1) * 128, :]))(tt),
                    writes=[BF("xrs")], tag=f"dxl{tt}")
            bld.add("vector",
                    (lambda tl: lambda e: e.tensor_tensor(
                        gasm[tl], gasm[tl], srb, OP.mult))(tl),
                    reads=[BF("gasm", tl), BF("srb")], writes=[BF("gasm", tl)],
                    tag=f"dmu{tt}")
            bld.add("vector",
                    (lambda tl: lambda e: e.tensor_tensor(
                        outs, gasm[tl], xrs, OP.add))(tl),
                    reads=[BF("gasm", tl), BF("xrs")], writes=[BF("outs")],
                    tag=f"dadd{tt}")
            oop = bld.dma("gpsimd",
                          (lambda tt: lambda e: e.dma_start(
                              out=y_out[tt * 128:(tt + 1) * 128, :],
                              in_=outs))(tt),
                          reads=[BF("outs")], tag=f"dout{tt}")
            bld.finalize_dep(oop)

    dbg_names = []
    tap_src = {"at": (at_dram, [("at_dram", h) for h in range(H)]),
               "sdr": (s_dram, [("s_dram", t) for t in range(NTT)]),
               "hdr": (h_dram, [("h_dram", i) for i in range(NIB)]),
               "matg": (mat["g"], [("mat", "g")]),
               "matu": (mat["u"], [("mat", "u")]),
               "matd": (mat["d"], [("mat", "d")]),
               "spkt": (sp_kt, [("sp_kt", h) for h in range(H)]),
               "spv": (sp_v, [("sp_v", h) for h in range(H)]),
               "spkd": (sp_kdec, [("sp_kdec", h) for h in range(H)]),
               "ccout": (cc_out, [("cc_out",)])}
    for name in debug_taps:
        src, keys = tap_src[name]
        d = nc.dram_tensor("dbg_" + name, list(src.shape), src.tensor.dtype,
                           kind="ExternalOutput").ap()
        dbg_names.append("dbg_" + name)
        op = bld.dma("gpsimd",
                     (lambda d, src: lambda e: e.dma_start(out=d, in_=src))(d, src),
                     reads=[BF(*k) for k in keys], tag=f"dbg_{name}")
        bld.finalize_dep(op)

    bld.emit(nc, es)
    return nc, es, dbg_names


def _pad_rows(a, rows):
    out = np.full((rows, a.shape[1]), 16, dtype=np.int32)
    out[:a.shape[0]] = a
    return out


def _pad_cols(a, cols):
    out = np.full((a.shape[0], cols), 16, dtype=np.int32)
    out[:, :a.shape[1]] = a
    return out


def kernel(L0_attn_x, L0_attn_Wq, L0_attn_Wk, L0_attn_Wv, L0_attn_Wo,
           L0_attn_rms_w, L0_ffn_lut, L0_ffn_gate_idx, L0_ffn_up_idx,
           L0_ffn_down_idx):
    global LAST_DEBUG
    x = np.ascontiguousarray(np.asarray(L0_attn_x, dtype=np.float32)).reshape(B * T, E)
    wq = np.ascontiguousarray(np.asarray(L0_attn_Wq, dtype=np.float32)[:, FULL_PERM])
    wk = np.ascontiguousarray(np.asarray(L0_attn_Wk, dtype=np.float32)[:, FULL_PERM])
    wv = np.ascontiguousarray(np.asarray(L0_attn_Wv, dtype=np.float32))
    wo = np.ascontiguousarray(np.asarray(L0_attn_Wo, dtype=np.float32))
    rmsw = np.ascontiguousarray(np.asarray(L0_attn_rms_w, dtype=np.float32)).reshape(1, E)
    lut = np.ascontiguousarray(np.asarray(L0_ffn_lut, dtype=np.float32)).reshape(1, 16)
    gidx = _pad_rows(np.asarray(L0_ffn_gate_idx, dtype=np.int32), NIBP * 128)
    uidx = _pad_rows(np.asarray(L0_ffn_up_idx, dtype=np.int32), NIBP * 128)
    didx = _pad_cols(np.asarray(L0_ffn_down_idx, dtype=np.int32), NIBP * 128)

    nc, es, dbg_names = build_program(DEBUG_TAPS)

    in_maps = []
    for c in range(NCORE):
        m = {
            "x": x[c * TOK:(c + 1) * TOK],
            "wq": wq, "wk": wk, "wv": wv, "wo": wo,
            "rmsw": rmsw, "lut": lut,
            "gidx": np.ascontiguousarray(gidx[c * IBB * 128:(c + 1) * IBB * 128, :]),
            "uidx": np.ascontiguousarray(uidx[c * IBB * 128:(c + 1) * IBB * 128, :]),
            "didx": np.ascontiguousarray(didx[:, c * IBB * 128:(c + 1) * IBB * 128]),
        }
        m.update(host_consts(c))
        in_maps.append(m)

    res = run_bass_kernel_spmd(nc, in_maps, core_ids=list(range(NCORE)))
    LAST_DEBUG = {name: [r[name] for r in res.results] for name in dbg_names}
    out = np.concatenate([r["y"] for r in res.results], axis=0)
    es.close()
    return out.reshape(B, T, E)


# version 20
# speedup vs baseline: 1.0248x; 1.0248x over previous
import math
import contextlib
import numpy as np
import ml_dtypes

import concourse.bass as bass
import concourse.mybir as mybir
from concourse.bass_utils import run_bass_kernel_spmd

F32 = mybir.dt.float32
BF16 = mybir.dt.bfloat16
I32 = mybir.dt.int32
OP = mybir.AluOpType
AF = mybir.ActivationFunctionType
bfloat16 = ml_dtypes.bfloat16

B, T, E = 2, 8192, 2048
H, D, CHUNK = 16, 128, 128
INTER = 5504
NIBP = 48
NIB = 43
NFB = 16
NCORE = 8
TOK = (B * T) // NCORE
NTT = TOK // 128
NCH = TOK // CHUNK
IBB = NIBP // NCORE
RMS_EPS = 1e-5
LN_EPS = 1e-5

DEBUG_TAPS = ()
LAST_DEBUG = {}

ROT_PERM = np.concatenate([np.arange(0, 32), np.arange(64, 96),
                           np.arange(32, 64), np.arange(96, 128)])
FULL_PERM = np.concatenate([h * 128 + ROT_PERM for h in range(H)])


def _build_slopes(n):
    def pow2(n):
        start = 2.0 ** (-2.0 ** (-(math.log2(n) - 3)))
        return [start * start ** i for i in range(n)]
    assert math.log2(n).is_integer()
    return np.asarray(pow2(n), dtype=np.float64)


_slopes = _build_slopes(H)
_r = np.exp(-_slopes)
R_C = (_r ** float(CHUNK)).astype(np.float64)


def host_consts(core_id):
    batch = core_id // (NCORE // B)
    strip = core_id % (NCORE // B)
    t0 = strip * TOK

    i = np.arange(CHUNK, dtype=np.float64)
    q_decay = _r[:, None] ** (i[None, :] + 1.0)
    k_decay = _r[:, None] ** (CHUNK - 1.0 - i[None, :])
    diff = i[:, None] - i[None, :]
    dm = np.where(diff >= 0,
                  _r[:, None, None] ** np.where(diff >= 0, diff, 0.0), 0.0)
    dmaskT = np.ascontiguousarray(dm.transpose(0, 2, 1))

    freq = (1.0 / 1024.0) ** np.linspace(0.0, 1.0, D // 4)
    t = np.arange(t0, t0 + TOK, dtype=np.float64)
    theta = t[None, :] * freq[:, None]

    coef = np.zeros((NCORE, H), dtype=np.float64)
    for rp in range(NCORE):
        bp, sp = rp // (NCORE // B), rp % (NCORE // B)
        if bp == batch and sp < strip:
            coef[rp] = R_C ** (NCH * (strip - sp - 1))

    def bf(x):
        return np.ascontiguousarray(x, dtype=np.float32).astype(bfloat16)

    def f32(x):
        return np.ascontiguousarray(x, dtype=np.float32)

    return {
        "c_dmask": bf(dmaskT),
        "c_qdec": bf(np.broadcast_to(q_decay[:, None, :], (H, 128, CHUNK))),
        "c_kdk": f32(k_decay.T),
        "c_rotc": bf(np.concatenate([np.cos(theta), np.cos(theta)], 0)),
        "c_rots": bf(np.concatenate([np.sin(theta), np.sin(theta)], 0)),
        "c_iden": bf(np.eye(128)),
        "c_ones1": f32(np.ones((1, 128))),
        "c_coef": f32(coef.reshape(1, NCORE * H)),
    }


class Buf:
    __slots__ = ("key", "writer", "readers", "pend")

    def __init__(self, key):
        self.key = key
        self.writer = None
        self.readers = []
        self.pend = []


class Opn:
    __slots__ = ("engine", "fn", "deps", "inc_amt", "is_async", "sem", "cum",
                 "needed", "tag", "klass")

    def __init__(self, engine, fn, is_async, tag, klass):
        self.engine = engine
        self.fn = fn
        self.deps = []
        self.is_async = is_async
        self.klass = klass
        self.inc_amt = 16 if klass == "d" else 1
        self.sem = None
        self.cum = None
        self.needed = klass != "c"
        self.tag = tag


ENGINES = ("sync", "scalar", "vector", "tensor", "gpsimd")


class Builder:
    def __init__(self):
        self.ops = []
        self.bufs = {}
        self.final_deps = []

    def buf(self, *key):
        b = self.bufs.get(key)
        if b is None:
            b = Buf(key)
            self.bufs[key] = b
        return b

    def alias(self, newkey, oldkeys):
        nb = self.buf(*newkey)
        for ok in oldkeys:
            ob = self.bufs.get(ok)
            if ob is None:
                continue
            if ob.writer is not None:
                nb.pend.append(ob.writer)
            nb.pend.extend(ob.readers)
            nb.pend.extend(ob.pend)
            ob.writer = None
            ob.readers = []
            ob.pend = []
        return nb

    def alias_group(self, newkeys, oldkeys):
        pend = []
        for ok in oldkeys:
            ob = self.bufs.get(ok)
            if ob is None:
                continue
            if ob.writer is not None:
                pend.append(ob.writer)
            pend.extend(ob.readers)
            pend.extend(ob.pend)
            ob.writer = None
            ob.readers = []
            ob.pend = []
        for nk in newkeys:
            nb = self.buf(*nk)
            nb.pend = nb.pend + list(pend)

    def add(self, engine, fn, reads=(), writes=(), is_async=False, tag="",
            klass=None):
        if klass is None:
            klass = "d" if is_async else "c"
        op = Opn(engine, fn, is_async, tag, klass)
        deps = []
        for b in reads:
            if b.writer is not None:
                deps.append(b.writer)
            b.readers.append(op)
        for b in writes:
            if b.writer is not None:
                deps.append(b.writer)
            deps.extend(b.readers)
            deps.extend(b.pend)
            b.writer = op
            b.readers = []
            b.pend = []
        seen = set()
        for d in deps:
            if d is op or id(d) in seen:
                continue
            seen.add(id(d))
            if (d.engine == "tensor" and engine == "tensor"
                    and not d.is_async and not is_async):
                continue
            op.deps.append(d)
            d.needed = True
        self.ops.append(op)
        return op

    def dma(self, queue, fn, reads=(), writes=(), tag=""):
        return self.add(queue, fn, reads, writes, is_async=True, tag=tag,
                        klass="d")

    def coll(self, fn, reads=(), writes=(), tag=""):
        return self.add("gpsimd", fn, reads, writes, is_async=True, tag=tag,
                        klass="x")

    def finalize_dep(self, op):
        self.final_deps.append(op)
        op.needed = True

    DMA_LANES = 10

    def emit(self, nc, es):
        streams = {e: [] for e in ENGINES}
        for op in self.ops:
            streams[op.engine].append(op)
        sem_handles = {}
        for e in ENGINES:
            dmas = [o for o in streams[e] if o.klass == "d"]
            nlanes = min(self.DMA_LANES, len(dmas)) or 0
            for lane in range(nlanes):
                sem_handles[(e, "d", lane)] = es.enter_context(
                    nc.semaphore(f"s_{e}_d{lane}"))
            lane_prev = {}
            lane_round = {}
            for i, op in enumerate(dmas):
                lane = i % nlanes
                r = lane_round.get(lane, 0)
                lane_round[lane] = r + 1
                op.sem = (e, "d", lane)
                op.cum = 16 * (r + 1)
                prev = lane_prev.get(lane)
                if prev is not None and prev not in op.deps:
                    op.deps.append(prev)
                lane_prev[lane] = op
            for kl in ("c", "x"):
                if any(o.klass == kl and o.needed for o in streams[e]):
                    sem_handles[(e, kl)] = es.enter_context(
                        nc.semaphore(f"s_{e}_{kl}"))
            cnt = {"c": 0, "x": 0}
            for op in streams[e]:
                if op.needed and op.klass in ("c", "x"):
                    cnt[op.klass] += op.inc_amt
                    op.sem = (e, op.klass)
                    op.cum = cnt[op.klass]

        block = es.enter_context(nc.Block())
        builder = self

        def make_emitter(ename, ops):
            def body(eng):
                watermark = {}
                for op in ops:
                    need = {}
                    for d in op.deps:
                        assert d.sem is not None, (d.tag, op.tag)
                        if d.cum > need.get(d.sem, -1):
                            need[d.sem] = d.cum
                    for k, v in sorted(need.items()):
                        if watermark.get(k, 0) >= v:
                            continue
                        eng.wait_ge(sem_handles[k], v)
                        watermark[k] = v
                    inst = op.fn(eng)
                    if op.needed:
                        assert inst is not None, op.tag
                        inst.then_inc(sem_handles[op.sem], op.inc_amt)
                if ename == "gpsimd":
                    need = {}
                    for d in builder.final_deps:
                        if d.cum > need.get(d.sem, -1):
                            need[d.sem] = d.cum
                    for k, v in sorted(need.items()):
                        if watermark.get(k, 0) >= v:
                            continue
                        eng.wait_ge(sem_handles[k], v)
                        watermark[k] = v
            return body

        for e in ENGINES:
            getattr(block, e)(make_emitter(e, streams[e]))


def build_program(debug_taps=()):
    nc = bass.Bass("TRN2", target_bir_lowering=False, debug=False)
    bld = Builder()
    es = contextlib.ExitStack()
    BF = bld.buf

    x_in = nc.dram_tensor("x", [TOK, E], F32, kind="ExternalInput").ap()
    w_ins = {nm: nc.dram_tensor(nm, [E, E], F32, kind="ExternalInput").ap()
             for nm in ("wq", "wk", "wv", "wo")}
    rmsw_in = nc.dram_tensor("rmsw", [1, E], F32, kind="ExternalInput").ap()
    lut_in = nc.dram_tensor("lut", [1, 16], F32, kind="ExternalInput").ap()
    gidx_in = nc.dram_tensor("gidx", [IBB * 128, E], I32, kind="ExternalInput").ap()
    uidx_in = nc.dram_tensor("uidx", [IBB * 128, E], I32, kind="ExternalInput").ap()
    didx_in = nc.dram_tensor("didx", [E, IBB * 128], I32, kind="ExternalInput").ap()
    cns = {}
    for name, shape, dt in [
            ("c_dmask", [H, 128, CHUNK], BF16),
            ("c_qdec", [H, 128, CHUNK], BF16),
            ("c_kdk", [128, H], F32),
            ("c_rotc", [64, TOK], BF16), ("c_rots", [64, TOK], BF16),
            ("c_iden", [128, 128], BF16),
            ("c_ones1", [1, 128], F32), ("c_coef", [1, NCORE * H], F32)]:
        cns[name] = nc.dram_tensor(name, shape, dt, kind="ExternalInput").ap()
    y_out = nc.dram_tensor("y", [TOK, E], F32, kind="ExternalOutput").ap()

    def idram(name, shape, dt):
        return nc.dram_tensor(name, shape, dt).ap()

    w_bf = {nm: idram(nm + "_bf", [E, E], BF16) for nm in ("wq", "wk", "wv", "wo")}
    w_cb = {nm: w_bf[nm].rearrange("(f p) n -> p f n", p=128)
            for nm in ("wq", "wk", "wv", "wo")}
    bld.add("gpsimd", lambda e: e.memset(cf[:, 464:465], LN_EPS),
            writes=[BF("c_eps")], tag="ms_eps")
    bld.add("gpsimd", lambda e: e.memset(cf[:, 465:466], 1.0 / E),
            writes=[BF("c_invE")], tag="ms_invE")
    bld.add("gpsimd", lambda e: e.memset(cf[:, 466:467], 1.0),
            writes=[BF("c_one128")], tag="ms_one")
    for nm in ("wk", "wv", "wq", "wo"):
        bld.dma("gpsimd",
                (lambda nm: lambda e: e.dma_start(out=w_bf[nm], in_=w_ins[nm]))(nm),
                writes=[BF("w_bf", nm)], tag=f"wcast_{nm}")
    mat_sh = {nm: idram("mat" + nm + "_sh", [IBB * 128, E], BF16)
              for nm in ("g", "u", "d")}
    mat = {nm: idram("mat" + nm, [NIBP * 128, E], BF16) for nm in ("g", "u", "d")}
    cc_send = [idram(f"cc_send{i}", [128, 8 * 128], F32) for i in range(2)]
    cc_out = [idram(f"cc_out{i}", [NCORE * 128, 8 * 128], F32) for i in range(2)]
    sp_kt = idram("sp_kt", [H, 128, TOK], BF16)
    sp_kdec = idram("sp_kdec", [H, 128, TOK], BF16)
    sp_v = idram("sp_v", [H, 128, TOK], BF16)
    at_dram = idram("at_dram", [H, 128, TOK], BF16)
    s_dram = idram("s_dram", [NTT, 128, E], BF16)
    h_dram = idram("h_dram", [NIB, 128, TOK], BF16)

    def sb(name, shape, dt):
        return es.enter_context(nc.sbuf_tensor(name, shape, dt))

    xnT = sb("xnT", [128, NFB, 2048], BF16)
    AR = sb("arena", [128, 44032], BF16)
    arB = sb("arB", [128, 6144], F32)
    send = sb("send", [128, H * 128], F32)
    cbuf = sb("cbuf", [128, 10496], BF16)
    cf = sb("cf", [128, 1024], F32)
    stats = sb("stats", [128, 64], F32)
    psf = [es.enter_context(nc.psum_tensor(f"psf{i}", [128, 512], F32))
           for i in range(6)]
    psb = [es.enter_context(nc.psum_tensor(f"psb{i}", [128, 1024], BF16))
           for i in range(2)]

    c_dmask = cbuf[:, 0:2048].rearrange("p (h n) -> p h n", h=H)
    c_qdec = cbuf[:, 2048:4096].rearrange("p (h n) -> p h n", h=H)
    c_rotc = cbuf[0:64, 4096:6144]
    c_rots = cbuf[0:64, 6144:8192]
    c_iden = cbuf[:, 8192:8320]
    v_kvb = cbuf[:, 8448:8576]
    v_qdecT = cbuf[:, 8576:8704]
    v_smask = cbuf[:, 8704:8832]
    c_kdk = cf[:, 0:16]
    c_rmsw = cf[:, 16:32]
    c_lutc = cf[:, 32:48]
    c_coefb = cf[:, 64:192]
    r_ones1 = cf[0:1, 192:320]
    r_lut = cf[0:1, 320:336]
    r_coef = cf[0:1, 336:464]
    r_rms = arB[0:1, 4096:6144]
    c_eps = cf[:, 464:465]
    c_invE = cf[:, 465:466]

    c_one128 = cf[:, 466:467]

    for tag, dst, src, wkey in [
            ("ld_dmask", c_dmask,
             cns["c_dmask"].rearrange("h a n -> a h n"), "c_dmask"),
            ("ld_qdec", c_qdec,
             cns["c_qdec"].rearrange("h a n -> a h n"), "c_qdec"),
            ("ld_rotc", c_rotc, cns["c_rotc"], "c_rotc"),
            ("ld_rots", c_rots, cns["c_rots"], "c_rots"),
            ("ld_iden", c_iden, cns["c_iden"], "c_iden"),
            ("ld_kdk", c_kdk, cns["c_kdk"], "c_kdk"),
            ("ld_ones1", r_ones1, cns["c_ones1"], "r_ones1"),
            ("ld_coef", r_coef, cns["c_coef"], "r_coef"),
            ("ld_lut", r_lut, lut_in, "r_lut"),
            ("ld_rms", r_rms, rmsw_in, "r_rms")]:
        bld.dma("sync", (lambda d, s: lambda e: e.dma_start(out=d, in_=s))(dst, src),
                writes=[BF(wkey)], tag=tag)

    bld.add("tensor", lambda e: e.matmul(psf[5][:, 0:16], r_ones1, r_lut,
                                         start=True, stop=True),
            reads=[BF("r_ones1"), BF("r_lut")], writes=[BF("psf", 5)], tag="bc_lut")
    bld.add("vector", lambda e: e.tensor_copy(c_lutc, psf[5][:, 0:16]),
            reads=[BF("psf", 5)], writes=[BF("c_lutc")], tag="ev_lut")
    bld.add("tensor", lambda e: e.matmul(psf[5][:, 0:128], r_ones1, r_coef,
                                         start=True, stop=True),
            reads=[BF("r_ones1"), BF("r_coef")], writes=[BF("psf", 5)], tag="bc_coef")
    bld.add("vector", lambda e: e.tensor_copy(c_coefb, psf[5][:, 0:128]),
            reads=[BF("psf", 5)], writes=[BF("c_coefb")], tag="ev_coef")
    for h in range(H):
        bld.add("tensor",
                (lambda h: lambda e: e.transpose(
                    psf[4][:, h:h + 1], r_rms[:, h * 128:(h + 1) * 128],
                    r_ones1[0:1, 0:1]))(h),
                reads=[BF("r_rms"), BF("r_ones1")], writes=[BF("psf", 4)],
                tag=f"tr_rms{h}")
    bld.add("vector", lambda e: e.tensor_copy(c_rmsw, psf[4][:, 0:16]),
            reads=[BF("psf", 4)], writes=[BF("c_rmsw")], tag="ev_rmsw")

    wstage = AR[:, 0:8192].rearrange("p (j n) -> p j n", j=4)
    xn_bf = AR[:, 8192:10240]
    idxt = [AR[:, 10240:12288], AR[:, 12288:14336]]
    Wt = AR[:, 14336:16384]
    term = AR[:, 16384:18432]
    stg_gu = AR[:, 18432:20480]
    stg_d = [AR[:, 20480 + k * 2048: 20480 + (k + 1) * 2048] for k in range(6)]
    kTh = AR[:, 32768:34816]
    vTh = AR[:, 34816:36864]
    kdech = AR[:, 36864:38912]
    vtokh = AR[:, 38912:40960]
    whead = AR[:, 40960:43008].rearrange("p (f n) -> p f n", f=NFB)

    def select_lut(dst_ap, src_ap, width, rkey, wkey, tagp, term_ap, term_key):
        bld.add("vector",
                (lambda dst_ap, src_ap: lambda e: e.tensor_scalar(
                    dst_ap, src_ap, 0.0, c_lutc[:, 0:1],
                    OP.is_equal, OP.mult))(dst_ap, src_ap),
                reads=[rkey, BF("c_lutc")], writes=[wkey], tag=tagp + "_s0")
        for j in range(1, 16):
            bld.add("vector",
                    (lambda j, src_ap, width, term_ap: lambda e: e.tensor_scalar(
                        term_ap[:, 0:width], src_ap, float(j), c_lutc[:, j:j + 1],
                        OP.is_equal, OP.mult))(j, src_ap, width, term_ap),
                    reads=[rkey, BF("c_lutc")], writes=[BF(*term_key)],
                    tag=tagp + f"_s{j}")
            bld.add("vector",
                    (lambda dst_ap, width, term_ap: lambda e: e.tensor_tensor(
                        dst_ap, term_ap[:, 0:width], dst_ap, OP.add))(dst_ap, width, term_ap),
                    reads=[BF(*term_key), wkey], writes=[wkey], tag=tagp + f"_a{j}")

    arBb = arB[:].bitcast(BF16)
    gu_wt = [Wt, AR[:, 20480:22528]]
    d_idxt = [arBb[:, 10752:11520], arBb[:, 11520:12288]]
    d_wt = [AR[:, 14336:15104], AR[:, 15104:15872]]
    d_term = AR[:, 16384:17152]

    def m_sel_gu(nm, idx_ap, ibb, mt):
        sl = mt % 2
        bld.dma("gpsimd",
                (lambda idx_ap, ibb, t: lambda e: e.dma_start(
                    out=t, in_=idx_ap[ibb * 128:(ibb + 1) * 128, :]))(
                        idx_ap, ibb, idxt[sl]),
                writes=[BF("idxt", sl)], tag=f"midx_{nm}{ibb}")
        select_lut(gu_wt[sl], idxt[sl], 2048, BF("idxt", sl), BF("Wt", sl),
                   f"msel_{nm}{ibb}", term, ("term",))

    def m_tr_gu(nm, ibb, mt):
        sl = mt % 2
        for fb in range(NFB):
            pb, pk = psb[fb % 2], BF("psb", fb % 2)
            bld.add("tensor",
                    (lambda fb, pb, sl=sl: lambda e: e.transpose(
                        pb[:, 256:384], gu_wt[sl][:, fb * 128:(fb + 1) * 128],
                        c_iden))(fb, pb),
                    reads=[BF("Wt", sl), BF("c_iden")], writes=[pk],
                    tag=f"mtr_{nm}{ibb}_{fb}")
            bld.add("scalar",
                    (lambda fb, pb: lambda e: e.copy(
                        stg_gu[:, fb * 128:(fb + 1) * 128], pb[:, 256:384]))(fb, pb),
                    reads=[pk], writes=[BF("stg_gu")], tag=f"mev_{nm}{ibb}_{fb}")
        bld.dma("scalar",
                (lambda nm, ibb: lambda e: e.dma_start(
                    out=mat_sh[nm][ibb * 128:(ibb + 1) * 128, :],
                    in_=stg_gu))(nm, ibb),
                reads=[BF("stg_gu")], writes=[BF("mat_sh", nm)],
                tag=f"mst_{nm}{ibb}")

    _d_alias = [False]

    def m_sel_d(ot, mt):
        sl = mt % 2
        if not _d_alias[0]:
            _d_alias[0] = True
            bld.alias_group(
                [("d_wt", 0), ("d_wt", 1), ("d_term",), ("d_idxt", 0),
                 ("d_idxt", 1)],
                [("atsl", 0), ("whead", 1)])
        bld.dma("gpsimd",
                (lambda ot, t: lambda e: e.dma_start(
                    out=t, in_=didx_in[ot * 128:(ot + 1) * 128, :]))(
                        ot, d_idxt[sl]),
                writes=[BF("d_idxt", sl)], tag=f"midx_d{ot}")
        select_lut(d_wt[sl], d_idxt[sl], IBB * 128, BF("d_idxt", sl),
                   BF("d_wt", sl), f"msel_d{ot}", d_term, ("d_term",))

    _d_alias2 = [False]

    def m_tr_d(ot, mt):
        sl = mt % 2
        if not _d_alias2[0]:
            _d_alias2[0] = True
            bld.alias_group(
                [("stg_d", k) for k in range(IBB)],
                [("atsl", 1), ("rstdb",), ("wohead", 0)])
        for k in range(IBB):
            pb, pk = psb[k % 2], BF("psb", k % 2)
            bld.add("tensor",
                    (lambda k, pb, sl=sl: lambda e: e.transpose(
                        pb[:, 256:384], d_wt[sl][:, k * 128:(k + 1) * 128],
                        c_iden))(k, pb),
                    reads=[BF("d_wt", sl), BF("c_iden")], writes=[pk],
                    tag=f"mtr_d{ot}_{k}")
            bld.add("scalar",
                    (lambda ot, k, pb: lambda e: e.copy(
                        stg_d[k][:, ot * 128:(ot + 1) * 128], pb[:, 256:384]))(ot, k, pb),
                    reads=[pk], writes=[BF("stg_d", k)], tag=f"mev_d{ot}_{k}")

    def m_ag(nm):
        bld.coll((lambda nm: lambda e: e.collective_compute(
            "AllGather", OP.bypass, replica_groups=[list(range(NCORE))],
            ins=[mat_sh[nm].opt()], outs=[mat[nm].opt()]))(nm),
            reads=[BF("mat_sh", nm)], writes=[BF("mat", nm)], tag=f"ag_{nm}")

    def m_fin():
        for k in range(IBB):
            bld.dma("scalar",
                    (lambda k: lambda e: e.dma_start(
                        out=mat_sh["d"][k * 128:(k + 1) * 128, :],
                        in_=stg_d[k]))(k),
                    reads=[BF("stg_d", k)], writes=[BF("mat_sh", "d")],
                    tag=f"mst_d{k}")
        m_ag("d")

    mats_units = []
    tiles = ([("gu", "g", gidx_in, ibb) for ibb in range(IBB)]
             + [("gu", "u", uidx_in, ibb) for ibb in range(IBB)]
             + [("d", None, None, ot) for ot in range(NFB)])
    sel_u, tr_u = [], []
    for mt, (kind, nm, ap_, i) in enumerate(tiles):
        if kind == "gu":
            sel_u.append((lambda nm=nm, ap_=ap_, i=i, mt=mt: m_sel_gu(nm, ap_, i, mt)))
            tr_u.append((lambda nm=nm, i=i, mt=mt: m_tr_gu(nm, i, mt)))
        else:
            sel_u.append((lambda i=i, mt=mt: m_sel_d(i, mt)))
            tr_u.append((lambda i=i, mt=mt: m_tr_d(i, mt)))
    NGU = 2 * IBB
    mats_tasks = [sel_u[0], sel_u[1]]
    for i in range(NGU):
        mats_tasks.append(tr_u[i])
        if i == IBB - 1:
            mats_tasks.append(lambda: m_ag("g"))
        if i == NGU - 1:
            mats_tasks.append(lambda: m_ag("u"))
        if i + 2 < NGU:
            mats_tasks.append(sel_u[i + 2])
    ffn_units = [sel_u[NGU], sel_u[NGU + 1]]
    for i in range(NGU, len(tiles)):
        ffn_units.append(tr_u[i])
        if i + 2 < len(tiles):
            ffn_units.append(sel_u[i + 2])
    ffn_units.append(m_fin)

    xt = [arB[:, 0:2048], arB[:, 2048:4096]]
    for tt in range(NTT):
        xk = BF("xt", tt % 2)
        bld.dma("sync",
                (lambda tt, b: lambda e: e.dma_start(
                    out=b, in_=x_in[tt * 128:(tt + 1) * 128, :]))(tt, xt[tt % 2]),
                writes=[xk], tag=f"xld{tt}")
        for g in range(4):
            bld.add("vector",
                    (lambda b, g: lambda e: e.bn_stats(
                        stats[:, g * 6:(g + 1) * 6],
                        b[:, g * 512:(g + 1) * 512]))(xt[tt % 2], g),
                    reads=[xk], writes=[BF("stats")], tag=f"bn{tt}_{g}")
        bld.add("vector", lambda e: e.bn_aggr(stats[:, 32:34], stats[:, 0:24]),
                reads=[BF("stats")], writes=[BF("mv")], tag=f"bnagg{tt}")
        bld.add("scalar", lambda e: e.activation(
            stats[:, 34:35], stats[:, 33:34], AF.Sqrt, bias=c_eps),
            reads=[BF("mv"), BF("c_eps")], writes=[BF("sd")], tag=f"sqrt{tt}")
        bld.add("vector", lambda e: e.reciprocal(stats[:, 35:36], stats[:, 34:35]),
                reads=[BF("sd")], writes=[BF("rstd")], tag=f"rcp{tt}")
        bld.add("vector",
                (lambda b: lambda e: e.tensor_scalar(
                    xn_bf, b, stats[:, 32:33], stats[:, 35:36],
                    OP.subtract, OP.mult))(xt[tt % 2]),
                reads=[xk, BF("mv"), BF("rstd")], writes=[BF("xnbf")],
                tag=f"norm{tt}")
        for fb in range(NFB):
            pb, pk = psb[fb % 2], BF("psb", fb % 2)
            bld.add("tensor",
                    (lambda fb, pb: lambda e: e.transpose(
                        pb[:, 0:128], xn_bf[:, fb * 128:(fb + 1) * 128],
                        c_iden))(fb, pb),
                    reads=[BF("xnbf"), BF("c_iden")], writes=[pk],
                    tag=f"trx{tt}_{fb}")
            bld.add("scalar",
                    (lambda fb, tt, pb: lambda e: e.copy(
                        xnT[:, fb, tt * 128:(tt + 1) * 128], pb[:, 0:128]))(fb, tt, pb),
                    reads=[pk], writes=[BF("xnT", fb, tt)], tag=f"evx{tt}_{fb}")

    for _ in range(2):
        mats_tasks.pop(0)()

    S32 = arB[:, 4096:4224]
    bld.alias(("S32",), [("r_rms",)])
    rot_s1 = kdech[32:64, :]
    rot_s2 = kdech[0:32, :]

    def proj_head(dst_ap, wnm, h, dkey, tag, whead_ap=None, whead_key=None,
                  skip_load=False):
        whead_ap = whead if whead_ap is None else whead_ap
        whead_key = ("whead",) if whead_key is None else whead_key
        if not skip_load:
            bld.dma("sync",
                    (lambda wnm, h, whead_ap: lambda e: e.dma_start(
                        out=whead_ap,
                        in_=w_cb[wnm][:, :, h * 128:(h + 1) * 128]))(wnm, h, whead_ap),
                    reads=[BF("w_bf", wnm)], writes=[BF(*whead_key)],
                    tag=tag + "_wld")
        for tg in range(4):
            pf, pk = psf[tg % 2], BF("psf", tg % 2)

            def group(e, tg=tg, pf=pf, whead_ap=whead_ap):
                r = None
                for fb in range(NFB):
                    r = e.matmul(pf[:], whead_ap[:, fb, :],
                                 xnT[:, fb, tg * 512:(tg + 1) * 512],
                                 start=(fb == 0), stop=(fb == NFB - 1))
                return r
            bld.add("tensor", group,
                    reads=[BF(*whead_key)] + [BF("xnT", fb, tg * 4 + q)
                                              for fb in range(NFB) for q in range(4)],
                    writes=[pk], tag=tag + f"_mm{tg}")
            bld.add("scalar",
                    (lambda tg, pf, dst_ap: lambda e: e.copy(
                        dst_ap[:, tg * 512:(tg + 1) * 512], pf[:]))(tg, pf, dst_ap),
                    reads=[pk], writes=[dkey], tag=tag + f"_ev{tg}")

    def rotary(dst_ap, dkey, tag, s1=None, s2=None, s1key=("rot_s1",),
               s2key=("rot_s2",), pre_alias=None):
        s1 = rot_s1 if s1 is None else s1
        s2 = rot_s2 if s2 is None else s2
        if pre_alias is not None:
            bld.alias_group([s1key, s2key], pre_alias)
        A = dst_ap[0:32, :]
        Bq = dst_ap[32:64, :]
        s1k, s2k = BF(*s1key), BF(*s2key)
        bld.add("vector",
                (lambda A, s1: lambda e: e.tensor_tensor(
                    s1, A, c_rots[0:32, :], OP.mult))(A, s1),
                reads=[dkey, BF("c_rots")], writes=[s1k], tag=tag + "_r1")
        bld.add("vector",
                (lambda Bq, s2: lambda e: e.tensor_tensor(
                    s2, Bq, c_rots[32:64, :], OP.mult))(Bq, s2),
                reads=[dkey, BF("c_rots")], writes=[s2k], tag=tag + "_r2")
        bld.add("vector",
                (lambda A: lambda e: e.tensor_tensor(
                    A, A, c_rotc[0:32, :], OP.mult))(A),
                reads=[dkey, BF("c_rotc")], writes=[dkey], tag=tag + "_r3")
        bld.add("vector",
                (lambda A, s2: lambda e: e.tensor_tensor(A, A, s2, OP.add))(A, s2),
                reads=[dkey, s2k], writes=[dkey], tag=tag + "_r4")
        bld.add("vector",
                (lambda Bq: lambda e: e.tensor_tensor(
                    Bq, Bq, c_rotc[32:64, :], OP.mult))(Bq),
                reads=[dkey, BF("c_rotc")], writes=[dkey], tag=tag + "_r5")
        bld.add("vector",
                (lambda Bq, s1: lambda e: e.tensor_tensor(
                    Bq, Bq, s1, OP.subtract))(Bq, s1),
                reads=[dkey, s1k], writes=[dkey], tag=tag + "_r6")

    kThB = arBb[:, 0:2048]
    vThB = arBb[:, 2048:4096]
    kdechB = arBb[:, 4096:6144]
    vtokhB = arBb[:, 6144:8192]
    S32_s = [arB[:, 4096:4224], arB[:, 4224:4352]]
    wheadB = arBb[:, 8704:10752].rearrange("p (f n) -> p f n", f=NFB)
    bld.alias_group(
        [("kTh", 1), ("vTh", 1), ("kdech", 1), ("vtokh", 1), ("whead", 1),
         ("S32", 0), ("S32", 1)],
        [("xt", 0), ("xt", 1), ("r_rms",)])
    bld.alias_group(
        [("kTh", 0), ("vTh", 0), ("kdech", 0), ("vtokh", 0), ("whead", 0)],
        [("kTh",), ("vTh",), ("kdech",), ("vtokh",), ("whead",)])
    kTh_s, vTh_s = [kTh, kThB], [vTh, vThB]
    kdech_s, vtokh_s = [kdech, kdechB], [vtokh, vtokhB]
    whead_s = [whead, wheadB]
    wheadv_s = [AR[:, 4096:6144].rearrange("p (f n) -> p f n", f=NFB),
                AR[:, 6144:8192].rearrange("p (f n) -> p f n", f=NFB)]
    bld.alias(("wheadv", 0), [("wstage", 2)])
    bld.alias(("wheadv", 1), [("wstage", 3)])

    def p1_prologue(h):
        par = h % 2
        kTh_, vTh_, kdech_ = kTh_s[par], vTh_s[par], kdech_s[par]
        kk, vk = BF("kTh", par), BF("vTh", par)
        proj_head(kTh_, "wk", h, kk, f"p1k{h}", whead_s[par], ("whead", par))
        rotary(kTh_, kk, f"rotk{h}", s1=kdech_[32:64, :], s2=kdech_[0:32, :],
               s1key=("rot_s1", par), s2key=("rot_s2", par),
               pre_alias=[("kdech", par)])
        proj_head(vTh_, "wv", h, vk, f"p1v{h}", wheadv_s[par], ("wheadv", par))
        bld.alias_group([("kdech", par)], [("rot_s1", par), ("rot_s2", par)])

    def p1_front(h, c):
        par = h % 2
        kTh_, vTh_ = kTh_s[par], vTh_s[par]
        kdech_, vtokh_ = kdech_s[par], vtokh_s[par]
        kk, vk = BF("kTh", par), BF("vTh", par)
        S32_ = S32_s[par]
        pb, pk = psf[4 + par], BF("psf", 4 + par)
        bld.add("tensor",
                (lambda c, pb, kTh_=kTh_: lambda e: e.matmul(
                    pb[:, 0:128], kTh_[:, c * 128:(c + 1) * 128], c_iden,
                    start=True, stop=True))(c, pb),
                reads=[kk, BF("c_iden")], writes=[pk], tag=f"p1trk{h}_{c}")
        bld.add("tensor",
                (lambda c, pb, vTh_=vTh_: lambda e: e.matmul(
                    pb[:, 128:256], vTh_[:, c * 128:(c + 1) * 128], c_iden,
                    start=True, stop=True))(c, pb),
                reads=[vk, BF("c_iden")], writes=[pk], tag=f"p1trv{h}_{c}")
        bld.add("scalar",
                (lambda c, h, pb, kdech_=kdech_: lambda e: e.activation(
                    kdech_[:, c * 128:(c + 1) * 128], pb[:, 0:128],
                    AF.Copy, scale=c_kdk[:, h:h + 1]))(c, h, pb),
                reads=[pk, BF("c_kdk")], writes=[BF("kdech", par)],
                tag=f"p1kd{h}_{c}")
        bld.add("scalar",
                (lambda c, pb, vtokh_=vtokh_: lambda e: e.copy(
                    vtokh_[:, c * 128:(c + 1) * 128], pb[:, 128:256]))(c, pb),
                reads=[pk], writes=[BF("vtokh", par)], tag=f"p1ve{h}_{c}")
    def p1_back(h, c):
        par = h % 2
        kdech_, vtokh_ = kdech_s[par], vtokh_s[par]
        S32_ = S32_s[par]
        kdvS = kdvS_p1[par]
        pf2, pk2 = psf[2 + par], BF("psf", 2 + par)
        bld.add("tensor",
                (lambda c, pf2, kdech_=kdech_, vtokh_=vtokh_: lambda e: e.matmul(
                    pf2[:, 0:128], kdech_[:, c * 128:(c + 1) * 128],
                    vtokh_[:, c * 128:(c + 1) * 128], start=True, stop=True))(
                    c, pf2),
                reads=[BF("kdech", par), BF("vtokh", par)], writes=[pk2],
                tag=f"p1kdv{h}_{c}")
        bld.add("scalar",
                (lambda pf2, kdvS=kdvS: lambda e: e.copy(
                    kdvS, pf2[:, 0:128]))(pf2),
                reads=[pk2], writes=[BF("kdvSp", par)], tag=f"p1ke{h}_{c}")
        if c == 0:
            bld.add("vector",
                    (lambda S32_=S32_, kdvS=kdvS: lambda e: e.tensor_copy(
                        S32_, kdvS))(),
                    reads=[BF("kdvSp", par)], writes=[BF("S32", par)],
                    tag=f"p1si{h}")
        else:
            bld.add("vector",
                    (lambda h, S32_=S32_, kdvS=kdvS:
                     lambda e: e.scalar_tensor_tensor(
                         S32_, S32_, float(R_C[h]), kdvS,
                         OP.mult, OP.add))(h),
                    reads=[BF("kdvSp", par), BF("S32", par)],
                    writes=[BF("S32", par)], tag=f"p1sa{h}_{c}")

    def p1_epilogue(h):
        par = h % 2
        kTh_, kdech_, vtokh_ = kTh_s[par], kdech_s[par], vtokh_s[par]
        bld.add("vector",
                (lambda h, S32_=S32_s[par]: lambda e: e.tensor_copy(
                    send[:, h * 128:(h + 1) * 128], S32_))(h),
                reads=[BF("S32", par)], writes=[BF("send", h)], tag=f"p1se{h}")
        bld.dma("scalar", (lambda h, kTh_=kTh_: lambda e: e.dma_start(
            out=sp_kt[h], in_=kTh_))(h), reads=[BF("kTh", par)],
            writes=[BF("sp_kt", h)], tag=f"spk{h}")
        bld.dma("scalar", (lambda h, kdech_=kdech_: lambda e: e.dma_start(
            out=sp_kdec[h], in_=kdech_))(h), reads=[BF("kdech", par)],
            writes=[BF("sp_kdec", h)], tag=f"spkd{h}")
        bld.dma("scalar", (lambda h, vtokh_=vtokh_: lambda e: e.dma_start(
            out=sp_v[h], in_=vtokh_))(h), reads=[BF("vtokh", par)],
            writes=[BF("sp_v", h)], tag=f"spv{h}")

    kdvS_p1 = [cf[:, 512:640], cf[:, 640:768]]

    for hp in range(H // 2):
        h0, h1 = 2 * hp, 2 * hp + 1
        p1_prologue(h0)
        p1_prologue(h1)
        for c in range(NCH):
            p1_front(h0, c)
            p1_front(h1, c)
            if c > 0:
                p1_back(h0, c - 1)
                p1_back(h1, c - 1)
        p1_back(h0, NCH - 1)
        p1_back(h1, NCH - 1)
        p1_epilogue(h0)
        p1_epilogue(h1)
        if hp in (3, 7):
            hb = hp // 4
            bld.dma("gpsimd",
                    (lambda hb: lambda e: e.dma_start(
                        out=cc_send[hb],
                        in_=send[:, hb * 1024:(hb + 1) * 1024]))(hb),
                    reads=[BF("send", hh) for hh in range(hb * 8, hb * 8 + 8)],
                    writes=[BF("cc_send", hb)], tag=f"ccst{hb}")
            bld.coll((lambda hb: lambda e: e.collective_compute(
                "AllGather", OP.bypass, replica_groups=[list(range(NCORE))],
                ins=[cc_send[hb].opt()],
                outs=[cc_out[hb].opt()]))(hb),
                reads=[BF("cc_send", hb)], writes=[BF("cc_out", hb)],
                tag=f"ag_state{hb}")
        while mats_tasks and len(mats_tasks) > max(0, 24 - (hp + 1) * 3):
            mats_tasks.pop(0)()

    while mats_tasks:
        mats_tasks.pop(0)()

    qTh = [AR[:, 0:2048], AR[:, 2048:4096]]
    kre_s = [AR[:, 4096:6144], AR[:, 32768:34816]]
    kdre_s = [AR[:, 6144:8192], AR[:, 34816:36864]]
    vre_s = [AR[:, 8192:10240], AR[:, 36864:38912]]
    _gb = AR[:, 10240:12288].bitcast(F32).rearrange("p (r n) -> p r n", r=NCORE)
    Gbuf_s = [_gb, _gb]
    atth_s = [AR[:, 12288:14336], AR[:, 38912:40960]]
    bld.alias_group([("qTh", 0), ("qTh", 1)], [("wstage", 0), ("wstage", 1)])
    bld.alias_group([("kre", 0), ("kdre", 0)], [("wheadv", 0), ("wheadv", 1)])
    bld.alias_group([("vre", 0)], [("xnbf",)])
    bld.alias_group([("Gbuf", 0), ("Gbuf", 1)], [("idxt", 0)])
    bld.alias_group([("atth", 0)], [("idxt", 1)])
    bld.alias_group(
        [("rot_s1", 2), ("rot_s2", 2)],
        [("stg_gu",), ("rot_s1", 0), ("rot_s2", 0),
         ("rot_s1", 1), ("rot_s2", 1)])
    bld.alias_group(
        [("kre", 1), ("kdre", 1), ("vre", 1), ("atth", 1)],
        [("kTh", 0), ("vTh", 0), ("kdech", 0), ("vtokh", 0)])
    rot2_s1 = AR[32:64, 18432:20480]
    rot2_s2 = AR[0:32, 18432:20480]
    qdecf_s = [AR[:, 20480:22528].rearrange("p (c n) -> p c n", c=NCH),
               AR[:, 22528:24576].rearrange("p (c n) -> p c n", c=NCH)]
    bld.alias_group([("qdecf", 0), ("qdecf", 1)], [("Wt", 1)])
    kvb_s = [[cbuf[:, 8448 + (par * 2 + cp) * 128:
                   8448 + (par * 2 + cp + 1) * 128] for cp in range(2)]
             for par in range(2)]
    smask_s = [[cbuf[:, 8960 + (par * 2 + cp) * 128:
                     8960 + (par * 2 + cp + 1) * 128] for cp in range(2)]
               for par in range(2)]

    sq = arB[:, 0:2048]
    ssq = arB[:, 2048:4096]
    bld.alias_group([("sq",), ("ssq",)],
                    [("kTh", 1), ("vTh", 1), ("kdech", 1), ("vtokh", 1)])

    def p2_prologue(h):
        par = h % 2
        qb, qbk = qTh[par], ("qTh", par)
        proj_head(qb, "wq", h, BF(*qbk), f"p2q{h}", whead_s[par], ("whead", par))
        rotary(qb, BF(*qbk), f"rotq{h}", s1=rot2_s1, s2=rot2_s2,
               s1key=("rot_s1", 2), s2key=("rot_s2", 2))
        for c in range(NCH):
            bld.add("vector",
                    (lambda h, c, qb, qd=qdecf_s[par]: lambda e: e.tensor_tensor(
                        qd[:, c, :], qb[:, c * 128:(c + 1) * 128],
                        c_qdec[:, h, :], OP.mult))(h, c, qb),
                    reads=[BF(*qbk), BF("c_qdec")], writes=[BF("qdecf", par)],
                    tag=f"qdf{h}_{c}")
        bld.dma("sync", (lambda h, kre=kre_s[par]: lambda e: e.dma_start(
            out=kre, in_=sp_kt[h]))(h),
                reads=[BF("sp_kt", h)], writes=[BF("kre", par)], tag=f"rlk{h}")
        bld.dma("sync", (lambda h, kd=kdre_s[par]: lambda e: e.dma_start(
            out=kd, in_=sp_kdec[h]))(h),
                reads=[BF("sp_kdec", h)], writes=[BF("kdre", par)], tag=f"rlkd{h}")
        bld.dma("sync", (lambda h, vr=vre_s[par]: lambda e: e.dma_start(
            out=vr, in_=sp_v[h]))(h),
                reads=[BF("sp_v", h)], writes=[BF("vre", par)], tag=f"rlv{h}")
        bld.dma("sync",
                (lambda h, Gbuf=Gbuf_s[par]: lambda e: e.dma_start(
                    out=Gbuf,
                    in_=cc_out[h // 8].rearrange(
                        "(r p) (g n) -> g p r n", p=128, g=8)[h % 8]))(h),
                reads=[BF("cc_out", h // 8)], writes=[BF("Gbuf", 0)],
                tag=f"gld{h}")
        S32_ = S32_s[par]
        bld.add("vector",
                (lambda h, Gbuf=Gbuf_s[par], S32_=S32_: lambda e: e.tensor_scalar(
                    S32_, Gbuf[:, 0, :], c_coefb[:, h:h + 1], None, OP.mult))(h),
                reads=[BF("Gbuf", 0), BF("c_coefb")], writes=[BF("S32", par)],
                tag=f"si0_{h}")
        for r in range(1, NCORE):
            bld.add("vector",
                    (lambda h, r, Gbuf=Gbuf_s[par], S32_=S32_:
                     lambda e: e.scalar_tensor_tensor(
                         S32_, Gbuf[:, r, :], c_coefb[:, r * H + h:r * H + h + 1],
                         S32_, OP.mult, OP.add))(h, r),
                    reads=[BF("Gbuf", 0), BF("c_coefb"), BF("S32", par)],
                    writes=[BF("S32", par)], tag=f"si{r}_{h}")

    kdvS_s = [cf[:, 512:640], cf[:, 640:768]]
    scS_s = [cf[:, 768:896], cf[:, 896:1024]]
    bld.alias_group([("kdvS", 0), ("kdvS", 1)], [("kdvSp", 0), ("kdvSp", 1)])

    def p2_front(h, c):
        par = h % 2
        qb, qbk = qTh[par], ("qTh", par)
        kre = kre_s[par]
        kdre = kdre_s[par].rearrange("p (c n) -> p c n", c=NCH)
        vre = vre_s[par].rearrange("p (c n) -> p c n", c=NCH)
        S32_ = S32_s[par]
        v_kvb, v_smask = kvb_s[par][c % 2], smask_s[par][c % 2]
        kvbk, smk = ("v_kvb", par, c % 2), ("v_smask", par, c % 2)
        kdvS, scS = kdvS_s[par], scS_s[par]
        pfk, pkk = psf[4 + par], BF("psf", 4 + par)
        pf0, pk0 = psf[0 + par], BF("psf", 0 + par)
        if c == 0:
            bld.add("scalar",
                    (lambda v_kvb=v_kvb, S32_=S32_: lambda e: e.copy(
                        v_kvb, S32_))(),
                    reads=[BF("S32", par)], writes=[BF(*kvbk)], tag=f"kvb{h}_{c}")
        else:
            bld.add("vector",
                    (lambda h, v_kvb=v_kvb, S32_=S32_, kdvS=kdvS:
                     lambda e: e.scalar_tensor_tensor(
                         v_kvb, S32_, float(R_C[h]), kdvS,
                         OP.mult, OP.add))(h),
                    reads=[BF("S32", par), BF("kdvS", par)], writes=[BF(*kvbk)],
                    tag=f"kvb{h}_{c}")
            bld.add("vector",
                    (lambda h, S32_=S32_, kdvS=kdvS:
                     lambda e: e.scalar_tensor_tensor(
                         S32_, S32_, float(R_C[h]), kdvS,
                         OP.mult, OP.add))(h),
                    reads=[BF("S32", par), BF("kdvS", par)],
                    writes=[BF("S32", par)], tag=f"sup{h}_{c}")
        bld.add("tensor",
                (lambda c, pf0, qb, kre=kre: lambda e: e.matmul(
                    pf0[:, 0:128], kre[:, c * 128:(c + 1) * 128],
                    qb[:, c * 128:(c + 1) * 128], start=True, stop=True))(
                    c, pf0, qb),
                reads=[BF("kre", par), BF(*qbk)], writes=[pk0], tag=f"sc{h}_{c}")
        bld.add("scalar",
                (lambda pf0, scS=scS: lambda e: e.copy(scS, pf0[:, 0:128]))(pf0),
                reads=[pk0], writes=[BF("scS", par)], tag=f"scev{h}_{c}")
        bld.add("vector",
                (lambda h, v_smask=v_smask, scS=scS: lambda e: e.tensor_tensor(
                    v_smask, scS, c_dmask[:, h, :], OP.mult))(h),
                reads=[BF("scS", par), BF("c_dmask")], writes=[BF(*smk)],
                tag=f"sm{h}_{c}")
        if c < NCH - 1:
            bld.add("tensor",
                    (lambda c, pfk=pfk, kdre=kdre, vre=vre: lambda e: e.matmul(
                        pfk[:, 0:128], kdre[:, c, :], vre[:, c, :],
                        start=True, stop=True))(c),
                    reads=[BF("kdre", par), BF("vre", par)], writes=[pkk],
                    tag=f"kdv2{h}_{c}")
            bld.add("scalar",
                    (lambda pfk=pfk, kdvS=kdvS: lambda e: e.copy(
                        kdvS, pfk[:, 0:128]))(),
                    reads=[pkk], writes=[BF("kdvS", par)], tag=f"kdve{h}_{c}")

    def p2_back(h, c):
        par = h % 2
        vre = vre_s[par].rearrange("p (c n) -> p c n", c=NCH)
        atth = atth_s[par]
        v_smask, v_kvb = smask_s[par][c % 2], kvb_s[par][c % 2]
        smk, kvbk = ("v_smask", par, c % 2), ("v_kvb", par, c % 2)
        pf1, pk1 = psf[2 + par], BF("psf", 2 + par)

        def outgrp(e, c=c, pf1=pf1, vre=vre, v_smask=v_smask, v_kvb=v_kvb,
                   qd=qdecf_s[par]):
            e.matmul(pf1[:, 0:128], vre[:, c, :], v_smask,
                     start=True, stop=False)
            return e.matmul(pf1[:, 0:128], v_kvb, qd[:, c, :],
                            start=False, stop=True)
        bld.add("tensor", outgrp,
                reads=[BF("vre", par), BF(*smk), BF(*kvbk), BF("qdecf", par)],
                writes=[pk1], tag=f"out{h}_{c}")
        bld.add("scalar",
                (lambda c, pf1, atth=atth: lambda e: e.copy(
                    atth[:, c * 128:(c + 1) * 128], pf1[:, 0:128]))(c, pf1),
                reads=[pk1], writes=[BF("atth", par)], tag=f"ate{h}_{c}")

    def p2_epilogue(h):
        par = h % 2
        atth = atth_s[par]
        bld.add("scalar", (lambda atth=atth: lambda e: e.square(sq, atth))(),
                reads=[BF("atth", par)], writes=[BF("sq")], tag=f"sq{h}")
        if h == 0:
            bld.add("vector", lambda e: e.tensor_copy(ssq, sq),
                    reads=[BF("sq")], writes=[BF("ssq")], tag=f"ssqc{h}")
        else:
            bld.add("vector", lambda e: e.tensor_tensor(ssq, sq, ssq, OP.add),
                    reads=[BF("sq"), BF("ssq")], writes=[BF("ssq")],
                    tag=f"ssqa{h}")
        bld.dma("scalar", (lambda h, atth=atth: lambda e: e.dma_start(
            out=at_dram[h], in_=atth))(h),
            reads=[BF("atth", par)], writes=[BF("at_dram", h)], tag=f"atsp{h}")

    for hp in range(H // 2):
        h0, h1 = 2 * hp, 2 * hp + 1
        p2_prologue(h0)
        p2_prologue(h1)
        for c in range(NCH):
            p2_front(h0, c)
            p2_front(h1, c)
            if c > 0:
                p2_back(h0, c - 1)
                p2_back(h1, c - 1)
        p2_back(h0, NCH - 1)
        p2_back(h1, NCH - 1)
        p2_epilogue(h0)
        p2_epilogue(h1)

    for g in range(4):
        bld.add("tensor",
                (lambda g: lambda e: e.matmul(
                    psf[5][0:1, 0:512], c_one128, ssq[:, g * 512:(g + 1) * 512],
                    start=True, stop=True))(g),
                reads=[BF("ssq"), BF("c_one128")], writes=[BF("psf", 5)],
                tag=f"rsum{g}")
        bld.add("vector",
                (lambda g: lambda e: e.tensor_copy(
                    sq[0:1, g * 512:(g + 1) * 512], psf[5][0:1, 0:512]))(g),
                reads=[BF("psf", 5)], writes=[BF("sqrow")], tag=f"rsev{g}")
    bld.add("scalar", lambda e: e.activation(ssq[0:1, :], sq[0:1, :], AF.Sqrt,
                                             bias=c_eps[0:1, :], scale=c_invE[0:1, :]),
            reads=[BF("sqrow"), BF("c_eps"), BF("c_invE")], writes=[BF("ssqrow")],
            tag="rsqrt")
    bld.add("vector", lambda e: e.reciprocal(sq[0:1, :], ssq[0:1, :]),
            reads=[BF("ssqrow")], writes=[BF("sqrow")], tag="rrecip")
    rstdb = AR[:, 28672:30720]
    bld.alias(("rstdb",), [("stg_d", 4)])
    for g in range(4):
        bld.add("tensor",
                (lambda g: lambda e: e.matmul(
                    psf[5][:, 0:512], r_ones1, sq[0:1, g * 512:(g + 1) * 512],
                    start=True, stop=True))(g),
                reads=[BF("sqrow"), BF("r_ones1")], writes=[BF("psf", 5)],
                tag=f"rbc{g}")
        bld.add("vector",
                (lambda g: lambda e: e.tensor_copy(
                    rstdb[:, g * 512:(g + 1) * 512], psf[5][:, 0:512]))(g),
                reads=[BF("psf", 5)], writes=[BF("rstdb")], tag=f"rbce{g}")

    wgb = [AR[:, k * 2048:(k + 1) * 2048].rearrange("p (f n) -> p f n", f=NFB)
           for k in range(2)]
    wub = [AR[:, 4096 + k * 2048: 4096 + (k + 1) * 2048]
           .rearrange("p (f n) -> p f n", f=NFB) for k in range(2)]
    sgb = [cbuf[:, 9472 + k * 512: 9472 + (k + 1) * 512] for k in range(2)]
    hst = [AR[:, 8192 + k * 2048: 8192 + (k + 1) * 2048] for k in range(2)]
    bld.alias_group(
        [("wgb", 0), ("wgb", 1), ("wub", 0), ("wub", 1), ("sgb", 0),
         ("sgb", 1), ("hst", 0), ("hst", 1)],
        [("qTh", 0), ("qTh", 1), ("kre", 0), ("kdre", 0), ("vre", 0),
         ("Gbuf", 0)])

    def ffn_ib(ib, drain_units=True):
        if drain_units:
            while ffn_units and len(ffn_units) > max(0, 35 - (ib + 1)):
                ffn_units.pop(0)()
        for nm, buf, bk in (("g", wgb[ib % 2], ("wgb", ib % 2)),
                            ("u", wub[ib % 2], ("wub", ib % 2))):
            bld.dma("sync",
                    (lambda nm, ib, buf: lambda e: e.dma_start(
                        out=buf.rearrange("p f n -> p (f n)"),
                        in_=mat[nm][ib * 128:(ib + 1) * 128, :]))(nm, ib, buf),
                    reads=[BF("mat", nm)], writes=[BF(*bk)], tag=f"f{nm}l{ib}")
        hk = BF("hst", ib % 2)
        for tg in range(4):
            pfg, pkg = psf[tg % 2], BF("psf", tg % 2)
            pfu, pku = psf[2 + tg % 2], BF("psf", 2 + tg % 2)

            def ggrp(e, buf=wgb[ib % 2], tg=tg, pfg=pfg):
                r = None
                for fb in range(NFB):
                    r = e.matmul(pfg[:], buf[:, fb, :],
                                 xnT[:, fb, tg * 512:(tg + 1) * 512],
                                 start=(fb == 0), stop=(fb == NFB - 1))
                return r
            bld.add("tensor", ggrp,
                    reads=[BF("wgb", ib % 2)] + [BF("xnT", fb, tg * 4 + q)
                                                 for fb in range(NFB)
                                                 for q in range(4)],
                    writes=[pkg], tag=f"fgmm{ib}_{tg}")
            bld.add("scalar",
                    (lambda tg, pfg: lambda e: e.activation(
                        sgb[tg % 2], pfg[:], AF.Silu))(tg, pfg),
                    reads=[pkg], writes=[BF("sgb", tg % 2)], tag=f"fsilu{ib}_{tg}")

            def ugrp(e, buf=wub[ib % 2], tg=tg, pfu=pfu):
                r = None
                for fb in range(NFB):
                    r = e.matmul(pfu[:], buf[:, fb, :],
                                 xnT[:, fb, tg * 512:(tg + 1) * 512],
                                 start=(fb == 0), stop=(fb == NFB - 1))
                return r
            bld.add("tensor", ugrp,
                    reads=[BF("wub", ib % 2)] + [BF("xnT", fb, tg * 4 + q)
                                                 for fb in range(NFB)
                                                 for q in range(4)],
                    writes=[pku], tag=f"fumm{ib}_{tg}")
            bld.add("vector",
                    (lambda tg, pfu, ib: lambda e: e.tensor_tensor(
                        hst[ib % 2][:, tg * 512:(tg + 1) * 512],
                        sgb[tg % 2], pfu[:], OP.mult))(tg, pfu, ib),
                    reads=[BF("sgb", tg % 2), pku], writes=[hk],
                    tag=f"fh{ib}_{tg}")
        bld.dma("scalar",
                (lambda ib, b: lambda e: e.dma_start(
                    out=h_dram[ib], in_=b))(ib, hst[ib % 2]),
                reads=[hk], writes=[BF("h_dram", ib)], tag=f"fhsp{ib}")

    ffn_ib_tasks = [(lambda ib=ib: ffn_ib(ib, drain_units=False))
                    for ib in range(NIB)]

    def drain_ffn(n):
        for _ in range(n):
            if ffn_ib_tasks:
                ffn_ib_tasks.pop(0)()

    atsl = [AR[:, 12288:20480].rearrange("p (f n) -> p f n", f=H),
            AR[:, 20480:28672].rearrange("p (f n) -> p f n", f=H)]
    wohead = [AR[:, 30720 + k * 2048: 30720 + (k + 1) * 2048]
              .rearrange("p (f n) -> p f n", f=NFB) for k in range(2)]
    sTb = [AR[:, 34816 + k * 512: 34816 + (k + 1) * 512] for k in range(2)]
    s_stage = [AR[:, 35840 + k * 2048: 35840 + (k + 1) * 2048] for k in range(4)]
    bld.alias_group(
        [("atsl", 0), ("atsl", 1), ("wohead", 0), ("wohead", 1),
         ("sT", 0), ("sT", 1), ("s_stage", 0), ("s_stage", 1),
         ("s_stage", 2), ("s_stage", 3)],
        [("atth", 0), ("atth", 1), ("Wt", 0), ("term",), ("qdecf", 0),
         ("qdecf", 1), ("rot_s1", 2), ("rot_s2", 2), ("kre", 1), ("kdre", 1),
         ("vre", 1), ("whead", 0)])

    for g4 in range(4):
        ak, ab = BF("atsl", g4 % 2), atsl[g4 % 2]
        bld.dma("sync",
                (lambda g4, ab: lambda e: e.dma_start(
                    out=ab,
                    in_=at_dram[:, :, g4 * 512:(g4 + 1) * 512]
                    .rearrange("h p n -> p h n")))(g4, ab),
                reads=[BF("at_dram", hh) for hh in range(H)], writes=[ak],
                tag=f"atrl{g4}")
        for h in range(H):
            bld.add("vector",
                    (lambda h, ab, g4: lambda e: e.scalar_tensor_tensor(
                        ab[:, h, :], ab[:, h, :], c_rmsw[:, h:h + 1],
                        rstdb[:, g4 * 512:(g4 + 1) * 512], OP.mult, OP.mult))(
                        h, ab, g4),
                    reads=[ak, BF("c_rmsw"), BF("rstdb")], writes=[ak],
                    tag=f"rsc{g4}_{h}")
        for ob in range(NFB):
            wk_, wb_ = BF("wohead", ob % 2), wohead[ob % 2]
            bld.dma("scalar",
                    (lambda ob, wb_: lambda e: e.dma_start(
                        out=wb_,
                        in_=w_cb["wo"][:, :, ob * 128:(ob + 1) * 128]))(ob, wb_),
                    reads=[BF("w_bf", "wo")], writes=[wk_], tag=f"wol{g4}_{ob}")
            pf, pk = psf[ob % 2], BF("psf", ob % 2)

            def wogrp(e, wb_=wb_, ab=ab, pf=pf):
                r = None
                for fb in range(NFB):
                    r = e.matmul(pf[:], wb_[:, fb, :], ab[:, fb, :],
                                 start=(fb == 0), stop=(fb == NFB - 1))
                return r
            bld.add("tensor", wogrp, reads=[wk_, ak], writes=[pk],
                    tag=f"womm{g4}_{ob}")
            sk, sb_ = BF("sT", ob % 2), sTb[ob % 2]
            bld.add("vector",
                    (lambda sb_, pf: lambda e: e.tensor_copy(sb_, pf[:]))(sb_, pf),
                    reads=[pk], writes=[sk], tag=f"woev{g4}_{ob}")
            for q in range(4):
                pb, pbk = psb[q % 2], BF("psb", q % 2)
                bld.add("tensor",
                        (lambda sb_, q, pb: lambda e: e.transpose(
                            pb[:, 0:128], sb_[:, q * 128:(q + 1) * 128],
                            c_iden))(sb_, q, pb),
                        reads=[sk, BF("c_iden")], writes=[pbk],
                        tag=f"wotr{g4}_{ob}_{q}")
                bld.add("vector",
                        (lambda q, ob, pb: lambda e: e.tensor_copy(
                            s_stage[q][:, ob * 128:(ob + 1) * 128],
                            pb[:, 0:128]))(q, ob, pb),
                        reads=[pbk], writes=[BF("s_stage", q)],
                        tag=f"wose{g4}_{ob}_{q}")
        for q in range(4):
            bld.dma("scalar",
                    (lambda g4, q: lambda e: e.dma_start(
                        out=s_dram[g4 * 4 + q], in_=s_stage[q]))(g4, q),
                    reads=[BF("s_stage", q)], writes=[BF("s_dram", g4 * 4 + q)],
                    tag=f"ssp{g4}_{q}")

    _nib_left = [NIB]
    while ffn_ib_tasks:
        ib_done = NIB - len(ffn_ib_tasks)
        while ffn_units and len(ffn_units) > max(0, 35 - (ib_done + 1)):
            ffn_units.pop(0)()
        ffn_ib_tasks.pop(0)()
    while ffn_units:
        ffn_units.pop(0)()
    hhalf = AR[:, 0:NIB * 1024].rearrange("p (i n) -> p i n", i=NIB)
    bld.alias_group(
        [("hhalf", 0), ("hhalf", 1)],
        [("wgb", 0), ("wgb", 1), ("wub", 0), ("wub", 1),
         ("sgb", 0), ("sgb", 1), ("hst", 0), ("hst", 1),
         ("atsl", 0), ("atsl", 1), ("rstdb",),
         ("wohead", 0), ("wohead", 1), ("sT", 0), ("sT", 1),
         ("s_stage", 0), ("s_stage", 1), ("s_stage", 2),
         ("s_stage", 3), ("d_wt", 0), ("d_wt", 1), ("d_term",)]
        + [("stg_d", k) for k in range(IBB)])
    xf = xnT[:].rearrange("p f n -> p (f n)")
    wdob = [xf[:, k * 5504:(k + 1) * 5504].rearrange("p (i n) -> p i n", i=NIB)
            for k in range(2)]
    gT = [xf[:, 11008 + k * 1024: 11008 + (k + 1) * 1024] for k in range(2)]
    gasm = [xf[:, 13056 + k * 2048: 13056 + (k + 1) * 2048] for k in range(8)]
    srb = xf[:, 29440:31488]
    xnT_keys = [("xnT", fb, tt) for fb in range(NFB) for tt in range(NTT)]
    bld.alias_group(
        [("wdob", 0), ("wdob", 1), ("gT", 0), ("gT", 1)]
        + [("gasm", k) for k in range(8)] + [("srb",)],
        xnT_keys)
    xrs = arB[:, 0:2048]
    outs = arB[:, 2048:4096]
    bld.alias_group([("xrs",), ("outs",)], [("sq",), ("ssq",)])

    matd_t = mat["d"].rearrange("(i p) n -> i p n", p=128)
    hq = [AR[:, 0:NIB * 512].rearrange("p (i n) -> p i n", i=NIB),
          AR[:, NIB * 512:NIB * 1024].rearrange("p (i n) -> p i n", i=NIB)]
    for qtr in range(4):
        qp = qtr % 2
        bld.dma("sync",
                (lambda qtr, qp: lambda e: e.dma_start(
                    out=hq[qp],
                    in_=h_dram[:, :, qtr * 512:(qtr + 1) * 512]
                    .rearrange("i p n -> p i n")))(qtr, qp),
                reads=[BF("h_dram", ib) for ib in range(NIB)] if qtr == 0 else [],
                writes=[BF("hhalf", qp)], tag=f"dhl{qtr}")
        for ob in range(NFB):
            wdk, wdb = BF("wdob", ob % 2), wdob[ob % 2]
            bld.dma("sync",
                    (lambda ob, wdb: lambda e: e.dma_start(
                        out=wdb,
                        in_=matd_t[0:NIB, :, ob * 128:(ob + 1) * 128]
                        .rearrange("i p n -> p i n")))(ob, wdb),
                    reads=[BF("mat", "d")], writes=[wdk], tag=f"dwl{qtr}_{ob}")
            pf, pk = psf[ob % 4], BF("psf", ob % 4)

            def dgrp(e, wdb=wdb, qp=qp, pf=pf):
                r = None
                for ib in range(NIB):
                    r = e.matmul(pf[:], wdb[:, ib, :], hq[qp][:, ib, :],
                                 start=(ib == 0), stop=(ib == NIB - 1))
                return r
            bld.add("tensor", dgrp, reads=[wdk, BF("hhalf", qp)],
                    writes=[pk], tag=f"dmm{qtr}_{ob}")
            gk = BF("gT", ob % 2)
            bld.add("scalar",
                    (lambda ob, pf: lambda e: e.activation(
                        gT[ob % 2][:, 0:512], pf[:], AF.Silu))(ob, pf),
                    reads=[pk], writes=[gk], tag=f"dsilu{qtr}_{ob}")
            for q in range(4):
                tl = qp * 4 + q
                pb, pbk = psb[q % 2], BF("psb", q % 2)
                bld.add("tensor",
                        (lambda ob, q, pb: lambda e: e.transpose(
                            pb[:, 0:128],
                            gT[ob % 2][:, q * 128:(q + 1) * 128], c_iden))(ob, q, pb),
                        reads=[gk, BF("c_iden")], writes=[pbk],
                        tag=f"dtr{qtr}_{ob}_{q}")
                bld.add("vector",
                        (lambda tl, ob, pb: lambda e: e.tensor_copy(
                            gasm[tl][:, ob * 128:(ob + 1) * 128],
                            pb[:, 0:128]))(tl, ob, pb),
                        reads=[pbk], writes=[BF("gasm", tl)],
                        tag=f"dge{qtr}_{ob}_{q}")
        for q in range(4):
            tl = qp * 4 + q
            tt = qtr * 4 + q
            bld.dma("scalar",
                    (lambda tt: lambda e: e.dma_start(
                        out=srb, in_=s_dram[tt]))(tt),
                    reads=[BF("s_dram", tt)], writes=[BF("srb")],
                    tag=f"dsl{tt}")
            bld.dma("sync",
                    (lambda tt: lambda e: e.dma_start(
                        out=xrs, in_=x_in[tt * 128:(tt + 1) * 128, :]))(tt),
                    writes=[BF("xrs")], tag=f"dxl{tt}")
            bld.add("vector",
                    (lambda tl: lambda e: e.tensor_tensor(
                        gasm[tl], gasm[tl], srb, OP.mult))(tl),
                    reads=[BF("gasm", tl), BF("srb")], writes=[BF("gasm", tl)],
                    tag=f"dmu{tt}")
            bld.add("vector",
                    (lambda tl: lambda e: e.tensor_tensor(
                        outs, gasm[tl], xrs, OP.add))(tl),
                    reads=[BF("gasm", tl), BF("xrs")], writes=[BF("outs")],
                    tag=f"dadd{tt}")
            oop = bld.dma("gpsimd",
                          (lambda tt: lambda e: e.dma_start(
                              out=y_out[tt * 128:(tt + 1) * 128, :],
                              in_=outs))(tt),
                          reads=[BF("outs")], tag=f"dout{tt}")
            bld.finalize_dep(oop)

    dbg_names = []
    tap_src = {"at": (at_dram, [("at_dram", h) for h in range(H)]),
               "sdr": (s_dram, [("s_dram", t) for t in range(NTT)]),
               "hdr": (h_dram, [("h_dram", i) for i in range(NIB)]),
               "matg": (mat["g"], [("mat", "g")]),
               "matu": (mat["u"], [("mat", "u")]),
               "matd": (mat["d"], [("mat", "d")]),
               "spkt": (sp_kt, [("sp_kt", h) for h in range(H)]),
               "spv": (sp_v, [("sp_v", h) for h in range(H)]),
               "spkd": (sp_kdec, [("sp_kdec", h) for h in range(H)]),
               "ccout": (cc_out, [("cc_out",)])}
    for name in debug_taps:
        src, keys = tap_src[name]
        d = nc.dram_tensor("dbg_" + name, list(src.shape), src.tensor.dtype,
                           kind="ExternalOutput").ap()
        dbg_names.append("dbg_" + name)
        op = bld.dma("gpsimd",
                     (lambda d, src: lambda e: e.dma_start(out=d, in_=src))(d, src),
                     reads=[BF(*k) for k in keys], tag=f"dbg_{name}")
        bld.finalize_dep(op)

    bld.emit(nc, es)
    return nc, es, dbg_names


def _pad_rows(a, rows):
    out = np.full((rows, a.shape[1]), 16, dtype=np.int32)
    out[:a.shape[0]] = a
    return out


def _pad_cols(a, cols):
    out = np.full((a.shape[0], cols), 16, dtype=np.int32)
    out[:, :a.shape[1]] = a
    return out


def kernel(L0_attn_x, L0_attn_Wq, L0_attn_Wk, L0_attn_Wv, L0_attn_Wo,
           L0_attn_rms_w, L0_ffn_lut, L0_ffn_gate_idx, L0_ffn_up_idx,
           L0_ffn_down_idx):
    global LAST_DEBUG
    x = np.ascontiguousarray(np.asarray(L0_attn_x, dtype=np.float32)).reshape(B * T, E)
    wq = np.ascontiguousarray(np.asarray(L0_attn_Wq, dtype=np.float32)[:, FULL_PERM])
    wk = np.ascontiguousarray(np.asarray(L0_attn_Wk, dtype=np.float32)[:, FULL_PERM])
    wv = np.ascontiguousarray(np.asarray(L0_attn_Wv, dtype=np.float32))
    wo = np.ascontiguousarray(np.asarray(L0_attn_Wo, dtype=np.float32))
    rmsw = np.ascontiguousarray(np.asarray(L0_attn_rms_w, dtype=np.float32)).reshape(1, E)
    lut = np.ascontiguousarray(np.asarray(L0_ffn_lut, dtype=np.float32)).reshape(1, 16)
    gidx = _pad_rows(np.asarray(L0_ffn_gate_idx, dtype=np.int32), NIBP * 128)
    uidx = _pad_rows(np.asarray(L0_ffn_up_idx, dtype=np.int32), NIBP * 128)
    didx = _pad_cols(np.asarray(L0_ffn_down_idx, dtype=np.int32), NIBP * 128)

    nc, es, dbg_names = build_program(DEBUG_TAPS)

    in_maps = []
    for c in range(NCORE):
        m = {
            "x": x[c * TOK:(c + 1) * TOK],
            "wq": wq, "wk": wk, "wv": wv, "wo": wo,
            "rmsw": rmsw, "lut": lut,
            "gidx": np.ascontiguousarray(gidx[c * IBB * 128:(c + 1) * IBB * 128, :]),
            "uidx": np.ascontiguousarray(uidx[c * IBB * 128:(c + 1) * IBB * 128, :]),
            "didx": np.ascontiguousarray(didx[:, c * IBB * 128:(c + 1) * IBB * 128]),
        }
        m.update(host_consts(c))
        in_maps.append(m)

    res = run_bass_kernel_spmd(nc, in_maps, core_ids=list(range(NCORE)))
    LAST_DEBUG = {name: [r[name] for r in res.results] for name in dbg_names}
    out = np.concatenate([r["y"] for r in res.results], axis=0)
    es.close()
    return out.reshape(B, T, E)
